# version 04
# baseline (speedup 1.0000x reference)
import numpy as np
import ml_dtypes
import concourse.bass as bass
import concourse.mybir as mybir
from concourse.bass_utils import run_bass_kernel_spmd
from contextlib import ExitStack

F32 = mybir.dt.float32
BF16 = mybir.dt.bfloat16
AF = mybir.ActivationFunctionType
ALU = mybir.AluOpType
AX = mybir.AxisListType.X

D = 2048
T = 2048
NT = 16
HT = 16
NH = 16
DH = 128
BLK = 256
NBLK = 8
E = 8
NI = 11
TH = 1024
CAP = 5120
BIG = 1.0e9

ENGINES = ("gp", "pe", "act", "dve", "sp")


class Sem:
    def __init__(self, nc, ctx, name):
        self.h = ctx.enter_context(nc.semaphore(name))
        self.n = 0

    def bump(self, k=1):
        self.n += k
        return self.n


class Prog:

    def __init__(self):
        self.streams = {e: [] for e in ENGINES}
        self.auto = {}

    def op(self, eng, fn, waits=(), incs=(), no_auto=False):
        waits = list(waits)
        incs = list(incs)
        a = None if no_auto else self.auto.get(eng)
        if a is not None:
            if a.n > 0:
                waits.append((a, a.n))
            if not any(sem is a for sem, _ in incs):
                incs.append((a, 1))
        for sem, k in incs:
            sem.bump(k)
        self.streams[eng].append((waits, fn, incs))

    def emit(self, block):
        def make(engine_name):
            def body(eng):
                for waits, fn, incs in self.streams[engine_name]:
                    for sem, val in waits:
                        eng.wait_ge(sem.h, val)
                    instr = fn(eng)
                    for sem, k in incs:
                        if instr is None:
                            instr = eng.nop()
                        instr = instr.then_inc(sem.h, k)
            return body
        block.gpsimd(make("gp"))
        block.tensor(make("pe"))
        block.scalar(make("act"))
        block.vector(make("dve"))
        block.sync(make("sp"))


def _bf(a):
    return np.ascontiguousarray(a).astype(ml_dtypes.bfloat16)


def stage_shared(W_qkvu, W_out, Wr, Wg, Wu, Wd, slope):
    st = {}
    W4 = np.asarray(W_qkvu, dtype=np.float32).reshape(4, NH, DH, HT, 128)
    st["wqkv"] = np.ascontiguousarray(W4[0:3].transpose(1, 0, 3, 4, 2))
    st["wuat"] = np.ascontiguousarray(W4[3].transpose(0, 2, 3, 1))
    st["wr"] = np.ascontiguousarray(
        np.asarray(Wr, dtype=np.float32).T.reshape(HT, 128, E))
    A = np.asarray(Wg, dtype=np.float32).transpose(0, 2, 1)
    wg5 = A.reshape(E, HT, 128, NI, 128).transpose(0, 3, 1, 2, 4)
    A = np.asarray(Wu, dtype=np.float32).transpose(0, 2, 1)
    wu5 = A.reshape(E, HT, 128, NI, 128).transpose(0, 3, 1, 2, 4)
    st["mwgu"] = np.ascontiguousarray(np.stack([wg5, wu5], axis=2))
    A = np.asarray(Wd, dtype=np.float32).transpose(0, 2, 1)
    st["mwd"] = np.ascontiguousarray(
        A.reshape(E, NI, 128, HT, 128).transpose(0, 3, 1, 2, 4))
    A = np.asarray(W_out, dtype=np.float32).T
    st["wout"] = np.ascontiguousarray(
        A.reshape(HT, 128, HT, 128).transpose(2, 0, 1, 3))
    s = np.asarray(slope, dtype=np.float32).reshape(1, NH)
    st["negslope"] = np.ascontiguousarray(np.tile(-s, (128, 1)))
    return st


def stage_consts():
    st = {}
    st["tri128"] = np.ascontiguousarray(
        (np.arange(128)[:, None] <= np.arange(128)[None, :]).astype(np.float32))
    st["tri16x"] = np.ascontiguousarray(
        (np.arange(16)[:, None] < np.arange(16)[None, :]).astype(np.float32))
    st["ones128b"] = _bf(np.ones((128, 128), np.float32))
    st["id128b"] = _bf(np.eye(128, dtype=np.float32))
    st["onescol"] = np.ones((128, 1), np.float32)
    st["ones16col"] = np.ones((16, 1), np.float32)
    st["ones16row"] = np.ones((1, 16), np.float32)
    st["ones8col"] = np.ones((8, 1), np.float32)
    st["ones1row"] = np.ones((1, 128), np.float32)
    st["ones1rowb"] = _bf(np.ones((1, 128), np.float32))
    st["arange8"] = np.ascontiguousarray(
        np.tile(np.arange(8, dtype=np.float32)[None, :], (128, 1)))
    i1 = np.arange(1, BLK + 1, dtype=np.float32)
    st["a1row"] = _bf(np.tile(i1[None, :], (128, 1)))
    a2 = np.stack([BLK - i1[0:128], BLK - i1[128:256]], axis=1)
    st["a2cols"] = np.ascontiguousarray(a2.astype(np.float32))
    st["c256"] = np.full((128, 1), float(BLK), np.float32)
    s_idx = np.arange(BLK)[:, None]
    t_idx = np.arange(BLK)[None, :]
    M = np.where(t_idx >= s_idx, (t_idx - s_idx).astype(np.float32), BIG)
    st["m1d"] = _bf(M[0:128])
    st["m2d"] = _bf(M[128:256])
    st["cvals"] = np.ascontiguousarray(np.tile(np.array(
        [0.0, 1.0, -0.5, 1e-5, 1e-6, 1.0 / 2048.0], np.float32)[None, :],
        (128, 1)))
    st["zeros16"] = np.zeros((16, 128), np.float32)
    return st


CONST_VALS = [0.0, 1.0, -0.5, 1e-5, 1e-6, 1.0 / 2048.0]


def build_nc(debug=(), upto=5, m_experts=E, m_halves=2, a0_heads=NH, a_heads=NH, a0_blocks=NBLK, skip=()):
    nc = bass.Bass("TRN2", target_bir_lowering=False)
    dbg = set(debug)

    dram = {}
    def din(name, shape, dt):
        t = nc.dram_tensor(name, shape, dt, kind="ExternalInput")
        dram[name] = t
        return t

    xT = din("xT", [D, T], F32)
    xTp = din("xTp", [D, T], F32)
    wqkv = din("wqkv", [NH, 3, HT, 128, 128], F32)
    wuat = din("wuat", [HT, HT, 128, 128], F32)
    wr = din("wr", [HT, 128, E], F32)
    mwgu = din("mwgu", [E, NI, 2, HT, 128, 128], F32)
    mwd = din("mwd", [E, HT, NI, 128, 128], F32)
    wout = din("wout", [HT, HT, 128, 128], F32)
    negslope = din("negslope", [128, NH], F32)
    below = din("below", [8, 1], F32)
    tri128 = din("tri128", [128, 128], F32)
    tri16x = din("tri16x", [16, 16], F32)
    ones128b = din("ones128b", [128, 128], BF16)
    id128b = din("id128b", [128, 128], BF16)
    onescol = din("onescol", [128, 1], F32)
    ones16col = din("ones16col", [16, 1], F32)
    ones16row = din("ones16row", [1, 16], F32)
    ones8col = din("ones8col", [8, 1], F32)
    ones1row = din("ones1row", [1, 128], F32)
    ones1rowb = din("ones1rowb", [1, 128], BF16)
    arange8 = din("arange8", [128, 8], F32)
    a1row = din("a1row", [128, BLK], BF16)
    a2cols = din("a2cols", [128, 2], F32)
    c256 = din("c256", [128, 1], F32)
    m1d = din("m1d", [128, BLK], BF16)
    m2d = din("m2d", [128, BLK], BF16)
    cvals = din("cvals", [128, 6], F32)
    zeros16 = din("zeros16", [16, 128], F32)

    out_d = nc.dram_tensor("out", [D, T], F32, kind="ExternalOutput")
    gspill = nc.dram_tensor("gspill", [HT, 128, T], BF16, kind="Internal")
    rsc = nc.dram_tensor("rsc", [16, 16], F32, kind="Internal")
    ccin = nc.dram_tensor("ccin", [16, 128], F32, kind="Internal")
    ccout = nc.dram_tensor("ccout", [128, 128], F32, kind="Internal")

    dbg_d = {}
    for name, shape, dt in [
        ("dbg_xn", [128, HT * T], BF16), ("dbg_comb", [128, NT * 8], F32),
        ("dbg_y", [128, HT * T], BF16), ("dbg_o", [128, NH * T], BF16),
        ("dbg_logits", [128, NT * 8], F32), ("dbg_mr", [128, 2 * T], BF16),
        ("dbg_seed", [128, NH * 128], BF16), ("dbg_gath", [8, 16], F32),
        ("dbg_crow1", [1, 256], F32), ("dbg_gu", [128, NI * TH], BF16),
        ("dbg_w", [128, 2048], BF16), ("dbg_gt", [128, 2 * TH], BF16),
        ("dbg_cbB", [128, TH], BF16), ("dbg_y0", [128, HT * T], BF16),
    ]:
        if name in dbg:
            dbg_d[name] = nc.dram_tensor(name, shape, dt, kind="ExternalOutput")

    ctx = ExitStack()
    with ctx:
        sb = lambda n, s, d: ctx.enter_context(nc.sbuf_tensor(n, s, d))
        psal = lambda n, s, d: ctx.enter_context(nc.psum_tensor(n, s, d))

        xn = sb("xn", [128, HT * T], BF16)
        abig = sb("abig", [128, HT * T], BF16)
        agu = sb("agu", [128, NI * TH], BF16)
        wstr = sb("wstr", [128, 8192], BF16)
        astr = sb("astr", [128, T], F32)
        astr2 = sb("astr2", [128, T], F32)
        mr = sb("mr", [128, 2 * T], BF16)
        combT = sb("combT", [8, T], BF16)
        crow = sb("crow", [1, TH], BF16)
        combB = sb("combB", [128, TH], BF16)
        kvseed = sb("kvseed", [128, NH * DH], BF16)
        kvf = sb("kvf", [128, DH], F32)
        kvb = sb("kvb", [128, 2 * DH], BF16)
        deck = sb("deck", [128, 2], F32)
        bdcol = sb("bdcol", [128, 1], F32)
        tri = sb("tri", [128, 128], F32)
        trix = sb("trix", [16, 16], F32)
        onesb = sb("onesb", [128, 128], BF16)
        idb = sb("idb", [128, 128], BF16)
        onec = sb("onec", [128, 1], F32)
        one16 = sb("one16", [16, 1], F32)
        one16r = sb("one16r", [1, 16], F32)
        one8 = sb("one8", [8, 1], F32)
        one1r = sb("one1r", [1, 128], F32)
        one1rb = sb("one1rb", [1, 128], BF16)
        ar8 = sb("ar8", [128, 8], F32)
        a1r = sb("a1r", [128, BLK], BF16)
        a2c = sb("a2c", [128, 2], F32)
        c256c = sb("c256c", [128, 1], F32)
        m1c = sb("m1c", [128, BLK], BF16)
        m2c = sb("m2c", [128, BLK], BF16)
        cv = sb("cv", [128, 6], F32)
        wrc = sb("wrc", [128, HT * E], F32)
        nsl = sb("nsl", [128, NH], F32)
        belowc = sb("belowc", [8, 1], F32)

        for i, v in enumerate(CONST_VALS):
            nc.const_aps.register(F32, v, cv[:, i:i + 1])

        abf = abig.ap().bitcast(F32)
        asb = astr.ap().bitcast(BF16)
        asb2 = astr2.ap().bitcast(BF16)
        logit = lambda: abf[:, 0:128]
        gatet = lambda: abf[:, 128:256]
        rtmpa = lambda: abf[:, 256:384]
        rtmpb = lambda: abf[:, 384:512]
        rcol = lambda: abf[:, 512:576]
        ind12 = lambda: abf[:, 576:832]
        compsb = lambda: abf[:, 832:960]
        cntrow = lambda: abf[0:1, 960:1216]
        cnts16 = lambda: abf[0:16, 1216:1232]
        gath = lambda: abf[0:8, 1232:1248]
        offtot = lambda: abf[0:1, 1248:1280]
        baserow = lambda: abf[0:1, 1280:1296]
        ccar = lambda: abf[0:16, 1296:1312]
        crow1 = lambda: abf[0:1, 1312:1568]
        sstk = lambda: abf[:, 1600:1728]
        compsbB = lambda: abf.bitcast(BF16)[:, 3456:3584]
        agv = agu.ap()
        qtile = lambda: agv[:, 0:2048]
        ktile = lambda: agv[:, 2048:4096]
        vtile = lambda: agv[:, 4096:6144]
        ktm = lambda s: agv[:, 6144 + s * 256: 6144 + (s + 1) * 256]
        vtok = lambda s: agv[:, 6656 + s * 256: 6656 + (s + 1) * 256]
        qdb = lambda s: agv[:, 7168 + s * 256: 7168 + (s + 1) * 256]
        qkdb = lambda s: agv[:, 7680 + s * 512: 7680 + (s + 1) * 512]
        decq = lambda: agv[:, 8704:8960]
        dect = lambda: agv[:, 8960:9472]
        sqg = lambda s: agv[:, s * 2048:(s + 1) * 2048]
        utg = lambda s: agv[:, 4096 + s * 2048: 4096 + (s + 1) * 2048]
        gutg = lambda s: agv[:, 8704 + (s % 1) * 0: 0]
        wsv = wstr.ap()
        wq_slot = lambda s: wsv[:, s * 2048:(s + 1) * 2048]
        woutg = lambda s: wsv[:, s * 2048:(s + 1) * 2048]
        uwg = lambda s: wsv[:, (2 + s) * 2048:(3 + s) * 2048]
        def wgu_sl(i):
            return (asb[:, i * 2048:(i + 1) * 2048] if i < 2
                    else asb2[:, (i - 2) * 2048:(i - 1) * 2048])
        wd_sl = lambda s: wsv[:, s * 1408:(s + 1) * 1408]
        gutv = lambda s: agv[:, s * 2048:(s + 1) * 2048]
        rbv = lambda: mr.ap()[:, 0:T]
        gbk = lambda: mr.ap()[:, T:2 * T]

        pA = psal("pA", [128, 2048], F32)
        pB = psal("pB", [128, 2048], F32)
        pAb = pA.ap().bitcast(BF16)
        pBb = pB.ap().bitcast(BF16)
        p_m = pA.ap()
        p_s2 = pB.ap()
        p_log = lambda s: pA.ap()[:, s * 512: s * 512 + 128]
        p_rank = pA.ap()[:, 128:144]
        p_sm = pA.ap()[0:1, 512:1024]
        p_c16 = pA.ap()[0:16, 1024:1040]
        p_ct = pB.ap()[0:8, :]
        p_cbB = pA.ap()[:, 0:1024]
        p_g = pA.ap()[:, 0:1024]
        p_u = pA.ap()[:, 1024:2048]
        p_y = lambda s: pB.ap()[:, s * 1024:(s + 1) * 1024]
        p_proj = pA.ap()
        p_tr = pAb[:, 0:512]
        p_trv = pAb[:, 1024:1536]
        p_A2 = pA.ap()[:, 1536:1664]
        p_o = lambda s: pB.ap()[:, s * 512:(s + 1) * 512][:, 0:256]
        p_qk = pB.ap()[:, 1024:1536]
        p_r = pA.ap()
        p_ug = pB.ap()
        p_s = pA.ap()

        sems = {}
        for nm in ["gp", "sp", "pe", "act", "dve", "cc", "bar", "st",
                   "xb0", "xb1", "x", "wg0", "wg1", "wu0", "wu1",
                   "wd0", "wd1", "cr", "wq0", "wq1", "wq2", "wq3",
                   "ug0", "ug1", "wo0", "wo1", "gb"]:
            sems[nm] = Sem(nc, ctx, "s_" + nm)
        S = type("S", (), sems)
        S_xb = [S.xb0, S.xb1]
        S_wg = [S.wg0, S.wg1]
        S_wu = [S.wu0, S.wu1]
        S_wd = [S.wd0, S.wd1]
        S_wq = [S.wq0, S.wq1, S.wq2, S.wq3]
        S_ug = [S.ug0, S.ug1]
        S_wo = [S.wo0, S.wo1]
        P = Prog()
        P.auto["dve"] = S.dve
        P.auto["act"] = S.act

        def barrier():
            for e in ENGINES:
                P.op(e, lambda g: g.drain(fusable=False))
                P.op(e, lambda g: g.nop(), incs=[(S.bar, 1)], no_auto=True)
            tgt = S.bar.n
            for e in ENGINES:
                P.op(e, lambda g: g.nop(), waits=[(S.bar, tgt)], no_auto=True)

        def wdma(dst, src, waits=(), sem=None):
            sem = sem or S.gp
            d = dst.rearrange("p (h c) -> p h c", c=128)
            sr = src.rearrange("h p c -> p h c")
            P.op("gp", lambda g, d=d, sr=sr: g.dma_start(d, sr),
                 waits=waits, incs=[(sem, 16)])
            return (sem, sem.n)

        def gdma(dst, src, waits=(), sem=None):
            sem = sem or S.gp
            P.op("gp", lambda g, d=dst, s=src: g.dma_start(d, s),
                 waits=waits, incs=[(sem, 16)])
            return (sem, sem.n) if sem is not S.gp else sem.n

        def sdma(dst, src, waits=(), sem=None):
            sem = sem or S.sp
            P.op("sp", lambda g, d=dst, s=src: g.dma_start(d, s),
                 waits=waits, incs=[(sem, 16)])
            return (sem, sem.n) if sem is not S.sp else sem.n

        wl = 0
        for dst, src in [
            (tri[:], tri128.ap()), (trix[:], tri16x.ap()),
            (onesb[:], ones128b.ap()), (idb[:], id128b.ap()),
            (onec[:], onescol.ap()), (one16[:], ones16col.ap()),
            (one16r[:], ones16row.ap()),
            (one8[:], ones8col.ap()), (one1r[:], ones1row.ap()),
            (one1rb[:], ones1rowb.ap()),
            (ar8[:], arange8.ap()), (a1r[:], a1row.ap()), (a2c[:], a2cols.ap()),
            (c256c[:], c256.ap()), (m1c[:], m1d.ap()), (m2c[:], m2d.ap()),
            (cv[:], cvals.ap()),
            (wrc[:].rearrange("p (h e) -> p h e", e=E),
             wr.ap().rearrange("h p e -> p h e")),
            (nsl[:], negslope.ap()), (belowc[:], below.ap()),
        ]:
            wl = sdma(dst, src)
        w_const = wl
        wz_ccin = gdma(ccin.ap(), zeros16.ap())
        for e in ENGINES:
            P.op(e, lambda g: g.nop(), waits=[(S.sp, w_const)])

        def ln_phase(x_dram, xn_arena, do_router):
            pe_done = []
            for ht in range(HT):
                sl = ht % 2
                xb = asb[:, sl * 2048:(sl + 1) * 2048]
                x2 = asb2[:, sl * 2048:(sl + 1) * 2048]
                waits = [(S.pe, pe_done[ht - 2])] if ht >= 2 else []
                w = gdma(xb, x_dram.ap()[ht * 128:(ht + 1) * 128, :],
                         waits=waits, sem=S_xb[sl])
                P.op("dve", lambda g, xb=xb, x2=x2: g.tensor_mul(x2, xb, xb),
                     waits=[w], incs=[(S.dve, 1)])
                dvn = S.dve.n
                for ch in range(4):
                    P.op("pe", lambda g, xb=xb, c=ch, h=ht: g.matmul(
                        p_m[:, c * 512:(c + 1) * 512], onesb[:],
                        xb[:, c * 512:(c + 1) * 512],
                        start=(h == 0), stop=(h == HT - 1)),
                        waits=[w] if ch == 0 else [])
                for ch in range(4):
                    P.op("pe", lambda g, x2=x2, c=ch, h=ht: g.matmul(
                        p_s2[:, c * 512:(c + 1) * 512], onesb[:],
                        x2[:, c * 512:(c + 1) * 512],
                        start=(h == 0), stop=(h == HT - 1)),
                        waits=[(S.dve, dvn)] if ch == 0 else [],
                        incs=[(S.pe, 1)] if ch == 3 else [])
                pe_done.append(S.pe.n)
            pe_stats = S.pe.n
            for ch in range(4):
                P.op("dve", lambda g, c=ch: g.tensor_scalar(
                    astr2.ap()[:, c * 512:(c + 1) * 512],
                    p_s2[:, c * 512:(c + 1) * 512],
                    1.0 / 2048.0, None, ALU.mult),
                    waits=[(S.pe, pe_stats)] if ch == 0 else [])
            for ch in range(4):
                P.op("dve", lambda g, c=ch: g.tensor_scalar(
                    astr.ap()[:, c * 512:(c + 1) * 512],
                    p_m[:, c * 512:(c + 1) * 512],
                    1.0 / 2048.0, None, ALU.mult))
            P.op("dve", lambda g: g.drain(fusable=False))
            for ch in range(4):
                P.op("dve", lambda g, c=ch: g.tensor_copy(
                    mr.ap()[:, c * 512:(c + 1) * 512],
                    astr.ap()[:, c * 512:(c + 1) * 512]))
            for ch in range(4):
                P.op("dve", lambda g, c=ch: g.tensor_mul(
                    abf[:, 2048 + c * 512: 2048 + (c + 1) * 512],
                    astr.ap()[:, c * 512:(c + 1) * 512],
                    astr.ap()[:, c * 512:(c + 1) * 512]))
            for ch in range(4):
                P.op("dve", lambda g, c=ch: g.tensor_sub(
                    astr2.ap()[:, c * 512:(c + 1) * 512],
                    astr2.ap()[:, c * 512:(c + 1) * 512],
                    abf[:, 2048 + c * 512: 2048 + (c + 1) * 512]),
                    incs=[(S.dve, 1)] if ch == 3 else [])
            dv_var = S.dve.n
            P.op("act", lambda g: g.activation(
                abf[:, 2048:4096].bitcast(F32) if False else abf[:, 2048:4096],
                astr2.ap()[:], AF.Ln, bias=1e-5, scale=1.0),
                waits=[(S.dve, dv_var)])
            P.op("act", lambda g: g.drain(fusable=False))
            P.op("act", lambda g: g.activation(
                mr.ap()[:, T:2 * T], abf[:, 2048:4096], AF.Exp,
                bias=0.0, scale=-0.5), incs=[(S.act, 1)])
            act_r = S.act.n
            pe_ht = []
            dv_lacc = {}
            for ht in range(HT):
                waits = [(S.act, act_r)]
                if ht >= 1:
                    waits = [(S.act, act_r + ht)]
                    if do_router:
                        waits.append((S.pe, pe_ht[ht - 1]))
                w = sdma(astr.ap()[:], x_dram.ap()[ht * 128:(ht + 1) * 128, :],
                         waits=waits, sem=S.x)
                P.op("dve", lambda g: g.tensor_sub(
                    astr2.ap()[:], astr.ap()[:], mr.ap()[:, 0:T]),
                    waits=[w, (S.act, act_r)])
                P.op("dve", lambda g: g.drain(fusable=False))
                P.op("dve", lambda g: g.tensor_mul(
                    astr.ap()[:], astr2.ap()[:], mr.ap()[:, T:2 * T]),
                    incs=[(S.dve, 1)])
                dv_xnf = S.dve.n
                P.op("act", lambda g, h=ht: g.copy(
                    xn_arena[:, h * T:(h + 1) * T], astr.ap()[:]),
                    waits=[(S.dve, dv_xnf)], incs=[(S.act, 1)])
                if do_router:
                    lsl = ht % 2
                    pw = [(S.dve, dv_xnf)]
                    if ht >= 2:
                        pw.append((S.dve, dv_lacc[ht - 2]))
                    for tt in range(NT):
                        P.op("pe", lambda g, h=ht, t=tt, q=lsl: g.matmul(
                            p_log(q)[:, t * E:(t + 1) * E],
                            astr.ap()[:, t * 128:(t + 1) * 128],
                            wrc[:, h * E:(h + 1) * E],
                            start=True, stop=True),
                            waits=pw if tt == 0 else [],
                            incs=[(S.pe, 1)] if tt == NT - 1 else [])
                    pe_l = S.pe.n
                    if ht == 0:
                        P.op("dve", lambda g, q=lsl: g.tensor_copy(
                            logit(), p_log(q)),
                            waits=[(S.pe, pe_l)], incs=[(S.dve, 1)])
                    else:
                        P.op("dve", lambda g, q=lsl: g.tensor_tensor(
                            logit(), logit(), p_log(q), ALU.add),
                            waits=[(S.pe, pe_l)], incs=[(S.dve, 1)])
                    dv_lacc[ht] = S.dve.n
                pe_ht.append(S.pe.n)
            return S.act.n

        act_xn = ln_phase(xT, xn.ap(), do_router=True)
        pe_logits = S.pe.n
        if "dbg_mr" in dbg_d:
            sdma(dbg_d["dbg_mr"].ap(), mr.ap()[:], waits=[(S.act, act_xn)])

        P.op("act", lambda g: g.nop(), waits=[(S.dve, S.dve.n)],
             incs=[(S.act, 1)])
        act_log = S.act.n
        LT = lambda tt: logit()[:, tt * E:(tt + 1) * E]
        GT = lambda tt: gatet()[:, tt * E:(tt + 1) * E]
        TA = lambda tt: rtmpa()[:, tt * E:(tt + 1) * E]
        TB = lambda tt: rtmpb()[:, tt * E:(tt + 1) * E]
        I1 = lambda tt: ind12()[:, tt * 16: tt * 16 + 8]
        I2 = lambda tt: ind12()[:, tt * 16 + 8: tt * 16 + 16]
        CM1 = lambda tt: rcol()[:, tt:tt + 1]
        CM2 = lambda tt: rcol()[:, NT + tt:NT + tt + 1]
        CI1 = lambda tt: rcol()[:, 2 * NT + tt:2 * NT + tt + 1]
        CI2 = lambda tt: rcol()[:, 3 * NT + tt:3 * NT + tt + 1]

        def dve_all(fn, waits_first=(), inc_last=True):
            for tt in range(NT):
                P.op("dve", lambda g, t=tt: fn(g, t),
                     waits=list(waits_first) if tt == 0 else [],
                     incs=[(S.dve, 1)] if (inc_last and tt == NT - 1) else [])
            return S.dve.n

        dve_all(lambda g, t: g.tensor_reduce(CM1(t), LT(t), AX, ALU.max,
                                             negate=True),
                waits_first=[(S.act, act_log)])
        dv1 = S.dve.n
        for tt in range(NT):
            P.op("act", lambda g, t=tt: g.activation(
                GT(t), LT(t), AF.Exp, bias=CM1(t), scale=1.0),
                waits=[(S.dve, dv1)] if tt == 0 else [],
                incs=[(S.act, 1)] if tt == NT - 1 else [])
        act2 = S.act.n
        dve_all(lambda g, t: g.tensor_reduce(CM2(t), GT(t), AX, ALU.add),
                waits_first=[(S.act, act2)], inc_last=False)
        dve_all(lambda g, t: g.reciprocal(CM2(t), CM2(t)), inc_last=False)
        dve_all(lambda g, t: g.tensor_scalar(GT(t), GT(t), CM2(t), None,
                                             ALU.mult), inc_last=False)
        dve_all(lambda g, t: g.tensor_reduce(CM1(t), GT(t), AX, ALU.max),
                inc_last=False)
        dve_all(lambda g, t: g.tensor_scalar(TA(t), GT(t), CM1(t), None,
                                             ALU.is_equal), inc_last=False)
        dve_all(lambda g, t: g.tensor_mul(TB(t), TA(t), ar8[:]), inc_last=False)
        dve_all(lambda g, t: g.tensor_scalar(TA(t), TA(t), -8.0, 8.0,
                                             ALU.mult, ALU.add), inc_last=False)
        dve_all(lambda g, t: g.tensor_add(TB(t), TB(t), TA(t)), inc_last=False)
        dve_all(lambda g, t: g.tensor_reduce(CI1(t), TB(t), AX, ALU.min),
                inc_last=False)
        dve_all(lambda g, t: g.tensor_scalar(I1(t), ar8[:], CI1(t), None,
                                             ALU.is_equal), inc_last=False)
        dve_all(lambda g, t: g.scalar_tensor_tensor(TA(t), I1(t), -1.0e30,
                                                    GT(t), ALU.mult, ALU.add),
                inc_last=False)
        dve_all(lambda g, t: g.tensor_reduce(CM2(t), TA(t), AX, ALU.max),
                inc_last=False)
        dve_all(lambda g, t: g.tensor_scalar(TA(t), TA(t), CM2(t), None,
                                             ALU.is_equal), inc_last=False)
        dve_all(lambda g, t: g.tensor_mul(TB(t), TA(t), ar8[:]), inc_last=False)
        dve_all(lambda g, t: g.tensor_scalar(TA(t), TA(t), -8.0, 8.0,
                                             ALU.mult, ALU.add), inc_last=False)
        dve_all(lambda g, t: g.tensor_add(TB(t), TB(t), TA(t)), inc_last=False)
        dve_all(lambda g, t: g.tensor_reduce(CI2(t), TB(t), AX, ALU.min),
                inc_last=False)
        dve_all(lambda g, t: g.tensor_scalar(I2(t), ar8[:], CI2(t), None,
                                             ALU.is_equal))
        dv_ind = S.dve.n

        for tt in range(NT):
            P.op("pe", lambda g, t=tt: g.matmul(
                p_sm[0:1, t * 16: t * 16 + 8], onec[:], I1(t),
                start=True, stop=True),
                waits=[(S.dve, dv_ind)] if tt == 0 else [])
            P.op("pe", lambda g, t=tt: g.matmul(
                p_sm[0:1, t * 16 + 8: t * 16 + 16], onec[:], I2(t),
                start=True, stop=True),
                incs=[(S.pe, 1)] if tt == NT - 1 else [])
        pe_cnt = S.pe.n
        P.op("act", lambda g: g.copy(cntrow(), p_sm[0:1, 0:256]),
             waits=[(S.pe, pe_cnt)], incs=[(S.act, 1)])
        act_cnt = S.act.n
        w1 = gdma(rsc.ap().rearrange("a b -> (a b)").rearrange("(o x) -> o x", o=1), cntrow(),
                  waits=[(S.act, act_cnt)])
        w2 = gdma(cnts16(), rsc.ap(), waits=[(S.gp, w1)])
        P.op("pe", lambda g: g.matmul(p_sm[0:1, 256:272], one16[:], cnts16(),
                                      start=True, stop=True),
             waits=[(S.gp, w2)], incs=[(S.pe, 1)])
        pe_tot = S.pe.n
        P.op("act", lambda g: g.copy(cntrow()[0:1, 0:16], p_sm[0:1, 256:272]),
             waits=[(S.pe, pe_tot)], incs=[(S.act, 1)])
        act_tot = S.act.n
        w3 = gdma(ccin.ap()[0:1, 0:16], cntrow()[0:1, 0:16],
                  waits=[(S.act, act_tot), (S.gp, w2)])
        P.op("gp", lambda g: g.collective_compute(
            "AllGather", ALU.bypass,
            replica_groups=[[0, 1, 2, 3, 4, 5, 6, 7]],
            ins=[ccin.ap().opt()], outs=[ccout.ap().opt()]),
            waits=[(S.gp, w3)], incs=[(S.cc, 1)])
        w4 = gdma(gath(), ccout.ap().rearrange("(c r) f -> c r f", r=16)[:, 0, 0:16],
                  waits=[(S.cc, 1)])
        if "dbg_gath" in dbg_d:
            sdma(dbg_d["dbg_gath"].ap(), gath(), waits=[(S.gp, w4)])
        P.op("pe", lambda g: g.matmul(p_sm[0:1, 272:288], belowc[:], gath(),
                                      start=True, stop=True),
             waits=[(S.gp, w4)])
        P.op("pe", lambda g: g.matmul(p_sm[0:1, 288:304], one8[:], gath(),
                                      start=True, stop=True),
             incs=[(S.pe, 1)])
        pe_off = S.pe.n
        P.op("act", lambda g: g.copy(offtot(), p_sm[0:1, 272:304]),
             waits=[(S.pe, pe_off)], incs=[(S.act, 1)])
        act_off = S.act.n
        P.op("dve", lambda g: g.tensor_copy(baserow()[0:1, 0:8],
                                            offtot()[0:1, 0:8]),
             waits=[(S.act, act_off)])
        P.op("dve", lambda g: g.tensor_add(
            baserow()[0:1, 8:16], offtot()[0:1, 8:16], offtot()[0:1, 16:24]),
            incs=[(S.dve, 1)])
        dv_base = S.dve.n
        P.op("pe", lambda g: g.matmul(p_c16, trix[:], cnts16(),
                                      start=True, stop=False),
             waits=[(S.dve, dv_base)])
        P.op("pe", lambda g: g.matmul(p_c16, one16r[:], baserow(),
                                      start=False, stop=True),
             incs=[(S.pe, 1)])
        pe_car = S.pe.n
        P.op("act", lambda g: g.copy(ccar(), p_c16),
             waits=[(S.pe, pe_car)], incs=[(S.act, 1)])
        act_car = S.act.n
        w5 = gdma(rsc.ap(), ccar(), waits=[(S.act, act_car)])
        w6 = gdma(crow1(), rsc.ap().rearrange("a b -> (a b)").rearrange("(o x) -> o x", o=1),
                  waits=[(S.gp, w5)])
        if "dbg_crow1" in dbg_d:
            sdma(dbg_d["dbg_crow1"].ap(), crow1(), waits=[(S.gp, w6)])

        p_rk = lambda t: pA.ap()[:, 256 + t * 16: 256 + (t + 1) * 16]
        KM = lambda t: abf[:, 2048 + t * 16: 2048 + (t + 1) * 16]
        KI = lambda t: abf[:, 2304 + t * 16: 2304 + (t + 1) * 16]
        KF = lambda t: abf[:, 2560 + 2 * t: 2560 + 2 * t + 2]
        WK = lambda t: abf[:, 2592 + 2 * t: 2592 + 2 * t + 2]
        C1 = lambda t: abf[:, 2624 + t * 8: 2624 + (t + 1) * 8]
        C2 = lambda t: abf[:, 2752 + t * 8: 2752 + (t + 1) * 8]
        for tt in range(NT):
            P.op("pe", lambda g, t=tt: g.matmul(
                p_rk(t), tri[:], ind12()[:, t * 16:(t + 1) * 16],
                start=True, stop=False),
                waits=[(S.gp, w6), (S.dve, dv_ind)] if tt == 0 else [])
            P.op("pe", lambda g, t=tt: g.matmul(
                p_rk(t), one1r[:], crow1()[0:1, t * 16:(t + 1) * 16],
                start=False, stop=True),
                incs=[(S.pe, 1)] if tt == NT - 1 else [])
        pe_rank = S.pe.n
        dve_all(lambda g, t: g.tensor_scalar(
            KM(t), p_rk(t), float(CAP), None, ALU.is_le),
            waits_first=[(S.pe, pe_rank)], inc_last=False)
        dve_all(lambda g, t: g.tensor_mul(
            KI(t), KM(t), ind12()[:, t * 16:(t + 1) * 16]), inc_last=False)
        dve_all(lambda g, t: g.tensor_reduce(
            KF(t)[:, 0:1], KI(t)[:, 0:8], AX, ALU.add), inc_last=False)
        dve_all(lambda g, t: g.tensor_reduce(
            KF(t)[:, 1:2], KI(t)[:, 8:16], AX, ALU.add), inc_last=False)
        dve_all(lambda g, t: g.tensor_mul(
            WK(t)[:, 0:1], CM1(t), KF(t)[:, 0:1]), inc_last=False)
        dve_all(lambda g, t: g.tensor_mul(
            WK(t)[:, 1:2], CM2(t), KF(t)[:, 1:2]), inc_last=False)
        dve_all(lambda g, t: g.tensor_scalar(
            C1(t), I1(t), WK(t)[:, 0:1], None, ALU.mult), inc_last=False)
        dve_all(lambda g, t: g.tensor_scalar(
            C2(t), I2(t), WK(t)[:, 1:2], None, ALU.mult), inc_last=False)
        dve_all(lambda g, t: g.tensor_add(
            compsb()[:, t * 8:(t + 1) * 8], C1(t), C2(t)))
        dv_comb = S.dve.n
        if "dbg_comb" in dbg_d:
            sdma(dbg_d["dbg_comb"].ap(), compsb(), waits=[(S.dve, dv_comb)])
        if "dbg_logits" in dbg_d:
            sdma(dbg_d["dbg_logits"].ap(), logit(), waits=[(S.dve, dv_comb)])
        if "dbg_xn" in dbg_d:
            sdma(dbg_d["dbg_xn"].ap(), xn.ap()[:], waits=[(S.act, act_xn)])
        P.op("dve", lambda g: g.tensor_copy(compsbB(), compsb()),
             waits=[(S.dve, dv_comb)], incs=[(S.dve, 1)])
        dv_cb = S.dve.n
        for tt in range(NT):
            P.op("pe", lambda g, t=tt: g.transpose(
                pBb[0:8, t * 128:(t + 1) * 128],
                compsbB()[:, t * 8:(t + 1) * 8], idb[:]),
                waits=[(S.dve, dv_cb)] if tt == 0 else [],
                incs=[(S.pe, 1)] if tt == NT - 1 else [])
        pe_ct = S.pe.n
        P.op("act", lambda g: g.copy(combT[:], pBb[0:8, 0:2048]),
             waits=[(S.pe, pe_ct)], incs=[(S.act, 1)])
        act_ct = S.act.n

        barrier()

        if upto < 2:
            with nc.Block() as block:
                P.emit(block)
            return nc
        pe_after_gu = {}
        gu_slot_free = {}
        wd_slot_free = {}
        y_evac = {}
        last_y_dve = 0
        act_gu_evac = {}
        crow_w = {}
        prev_pe_cbB = 0
        prev_act_cbB = 0


        gu_dma = {}
        wd_dma = {}
        dv_prev_gu_all = 0

        for hh in range(m_halves):
            tof = hh * TH
            for e in range(m_experts):
                wcr = gdma(crow[0:1, 0:TH],
                           combT[e:e + 1, tof:tof + TH],
                           waits=[(S.act, act_ct)] if (hh == 0 and e == 0)
                           else [(S.dve, last_y_dve)], sem=S.cr)
                cb_waits = [wcr]
                if prev_act_cbB:
                    cb_waits.append((S.act, S.act.n))
                for ch in range(2):
                    P.op("pe", lambda g, c=ch: g.matmul(
                        p_cbB[:, c * 512:(c + 1) * 512], one1rb[:],
                        crow[0:1, c * 512:(c + 1) * 512],
                        start=True, stop=True),
                        waits=cb_waits if ch == 0 else [],
                        incs=[(S.pe, 1)] if ch == 1 else [])
                pe_cbB = S.pe.n
                P.op("act", lambda g: g.copy(combB[:], p_cbB),
                     waits=[(S.pe, pe_cbB)], incs=[(S.act, 1)])
                prev_act_cbB = S.act.n
                act_cbB = S.act.n

                for it in range(NI):
                    slot = it % 2
                    key = (hh, e, it)
                    wts = []
                    if slot in pe_after_gu:
                        wts.append((S.pe, pe_after_gu[slot]))
                    wg = wdma(wgu_sl(slot * 2), mwgu.ap()[e, it, 0],
                              waits=wts, sem=S_wg[slot])
                    if hh == 0 and e == 0 and it == 0 and "dbg_w" in dbg_d:
                        sdma(dbg_d["dbg_w"].ap(), wgu_sl(0), waits=[wg])
                    wu = wdma(wgu_sl(slot * 2 + 1), mwgu.ap()[e, it, 1],
                              sem=S_wu[slot])
                    gw = [wg, (S.act, act_cbB)]
                    if (hh, e, it - 1) in act_gu_evac:
                        gw.append((S.act, act_gu_evac[(hh, e, it - 1)]))
                    for ch in range(2):
                        for ht in range(HT):
                            P.op("pe", lambda g, c=ch, h=ht, s=slot, o=tof:
                                 g.matmul(
                                p_g[:, c * 512:(c + 1) * 512],
                                wgu_sl(s * 2)[:, h * 128:(h + 1) * 128],
                                xn.ap()[:, h * T + o + c * 512:
                                        h * T + o + (c + 1) * 512],
                                start=(h == 0), stop=(h == HT - 1)),
                                waits=gw if (ch == 0 and ht == 0) else [])
                    for ch in range(2):
                        for ht in range(HT):
                            P.op("pe", lambda g, c=ch, h=ht, s=slot, o=tof:
                                 g.matmul(
                                p_u[:, c * 512:(c + 1) * 512],
                                wgu_sl(s * 2 + 1)[:, h * 128:(h + 1) * 128],
                                xn.ap()[:, h * T + o + c * 512:
                                        h * T + o + (c + 1) * 512],
                                start=(h == 0), stop=(h == HT - 1)),
                                waits=[wu] if (ch == 0 and ht == 0)
                                else [],
                                incs=[(S.pe, 1)] if (ch == 1 and ht == HT - 1)
                                else [])
                    pe_after_gu[slot] = S.pe.n
                    pe_gu = S.pe.n
                    gsl = it % 2
                    gtmp = wsv[:, (4 + gsl) * 1024:(5 + gsl) * 1024]
                    utmp = wsv[:, (6 + gsl) * 1024:(7 + gsl) * 1024]
                    aw2 = [(S.pe, pe_gu)]
                    if (hh, e, it - 2) in gu_dma:
                        aw2.append((S.dve, gu_dma[(hh, e, it - 2)]))
                    elif it < 2 and (e > 0 or hh > 0):
                        aw2.append((S.dve, dv_prev_gu_all))
                    P.op("act", lambda g, gt=gtmp: g.activation(
                        gt, p_g[:, 0:1024], AF.Silu, bias=0.0, scale=1.0),
                        waits=aw2)
                    P.op("act", lambda g, ut=utmp: g.copy(ut, p_u[:, 0:1024]),
                         incs=[(S.act, 1)])
                    act_gu_evac[key] = S.act.n
                    a_gu = S.act.n
                    if hh == 0 and e == 0 and it == 0 and "dbg_gt" in dbg_d:
                        sdma(dbg_d["dbg_gt"].ap()[:, 0:TH], gtmp,
                             waits=[(S.act, a_gu)])
                        sdma(dbg_d["dbg_gt"].ap()[:, TH:2 * TH], utmp,
                             waits=[(S.act, a_gu)])
                    P.op("dve", lambda g, gt=gtmp, ut=utmp, i=it: g.tensor_mul(
                        agv[:, i * TH:(i + 1) * TH], gt, ut),
                        waits=[(S.act, a_gu)])
                    P.op("dve", lambda g: g.drain(fusable=False))
                    P.op("dve", lambda g, i=it: g.tensor_mul(
                        agv[:, i * TH:(i + 1) * TH],
                        agv[:, i * TH:(i + 1) * TH], combB[:]),
                        incs=[(S.dve, 1)])
                    gu_dma[key] = S.dve.n
                dv_gu_all = S.dve.n
                dv_prev_gu_all = dv_gu_all
                if hh == 0 and e == 0:
                    if "dbg_gu" in dbg_d:
                        sdma(dbg_d["dbg_gu"].ap(), agv[:, 0:NI * TH],
                             waits=[(S.dve, dv_gu_all)])
                    if "dbg_cbB" in dbg_d:
                        sdma(dbg_d["dbg_cbB"].ap(), combB[:],
                             waits=[(S.dve, dv_gu_all)])

                for ht in range(HT):
                    slot = ht % 2
                    wts = []
                    if slot in wd_slot_free:
                        wts.append((S.pe, wd_slot_free[slot]))
                    wd = wdma(wd_sl(slot), mwd.ap()[e, ht], waits=wts,
                              sem=S_wd[slot])
                    ysl = ht % 2
                    yw = [wd, (S.dve, dv_gu_all)]
                    if (hh, e, ht - 2) in y_evac:
                        yw.append((S.dve, y_evac[(hh, e, ht - 2)]))
                    elif e > 0 and (hh, e - 1, HT - 2 + (ht % 2)) in y_evac:
                        yw.append((S.dve, y_evac[(hh, e - 1, HT - 2 + (ht % 2))]))
                    for ch in range(2):
                        for it in range(NI):
                            P.op("pe", lambda g, c=ch, i=it, s=slot: g.matmul(
                                p_y(s)[:, c * 512:(c + 1) * 512],
                                wd_sl(s)[:, i * 128:(i + 1) * 128],
                                agv[:, i * TH + c * 512: i * TH + (c + 1) * 512],
                                start=(i == 0), stop=(i == NI - 1)),
                                waits=yw if (ch == 0 and it == 0) else [],
                                incs=[(S.pe, 1)] if (ch == 1 and it == NI - 1)
                                else [])
                    wd_slot_free[slot] = S.pe.n
                    pe_y = S.pe.n
                    ytile = abig.ap()[:, ht * T + tof: ht * T + tof + TH]
                    if e == 0:
                        P.op("dve", lambda g, yt=ytile, s=ysl: g.tensor_copy(
                            yt, p_y(s)), waits=[(S.pe, pe_y)],
                            incs=[(S.dve, 1)])
                    else:
                        P.op("dve", lambda g, yt=ytile, s=ysl: g.tensor_tensor(
                            yt, yt, p_y(s), ALU.add), waits=[(S.pe, pe_y)],
                            incs=[(S.dve, 1)])
                    y_evac[(hh, e, ht)] = S.dve.n
                last_y_dve = S.dve.n
                if hh == 0 and e == 0 and "dbg_y0" in dbg_d:
                    sdma(dbg_d["dbg_y0"].ap(), abig.ap()[:],
                         waits=[(S.dve, last_y_dve)])

        dv_m_done = S.dve.n
        for ht in (range(HT) if (m_halves == 2 and m_experts == E) else []):
            P.op("act", lambda g, h=ht: g.activation(
                abig.ap()[:, h * T:(h + 1) * T], abig.ap()[:, h * T:(h + 1) * T],
                AF.Silu, bias=0.0, scale=1.0),
                waits=[(S.dve, dv_m_done)] if ht == 0 else [],
                incs=[(S.act, 1)])
            a = S.act.n
            sdma(gspill.ap()[ht], abig.ap()[:, ht * T:(ht + 1) * T],
                 waits=[(S.act, a)])
        w_spill = S.sp.n
        if "dbg_y" in dbg_d:
            sdma(dbg_d["dbg_y"].ap(), abig.ap()[:], waits=[(S.act, S.act.n)])

        barrier()

        if upto < 3:
            with nc.Block() as block:
                P.emit(block)
            return nc
        for e in ("act", "dve", "pe", "gp", "sp"):
            P.op(e, lambda g: g.nop(), waits=[(S.sp, S.sp.n)])
        act_xnp = ln_phase(xTp, abig.ap(), do_router=False)
        def head_weights(head, feats, waits):
            ws = []
            for f, slot in feats:
                ws.append(wdma(wq_slot(slot), wqkv.ap()[head, f],
                               waits=waits, sem=S_wq[slot]))
                waits = []
            return ws

        def project(dst, wslot, src_arena, pe_waits, sil):
            for ch in range(4):
                for ht in range(HT):
                    P.op("pe", lambda g, c=ch, h=ht: g.matmul(
                        p_proj[:, c * 512:(c + 1) * 512],
                        wq_slot(wslot)[:, h * 128:(h + 1) * 128],
                        src_arena[:, h * T + c * 512: h * T + (c + 1) * 512],
                        start=(h == 0), stop=(h == HT - 1)),
                        waits=pe_waits if (ch == 0 and ht == 0) else [],
                        incs=[(S.pe, 1)] if (ch == 3 and ht == HT - 1) else [])
            pe_p = S.pe.n
            if sil:
                P.op("act", lambda g, d=dst: g.activation(
                    d, p_proj[:], AF.Silu, bias=0.0, scale=1.0),
                    waits=[(S.pe, pe_p)], incs=[(S.act, 1)])
            else:
                P.op("act", lambda g, d=dst: g.copy(d, p_proj[:]),
                     waits=[(S.pe, pe_p)], incs=[(S.act, 1)])
            return S.act.n, pe_p

        def head_decays(head, need_q):
            nscol = nsl[:, head:head + 1]
            if "dec" in skip:
                P.op("act", lambda g: g.nop(), incs=[(S.act, 1)], no_auto=True)
                return S.act.n
            if need_q:
                P.op("act", lambda g, ns=nscol: g.activation(
                    decq(), a1r[:], AF.Exp, bias=0.0, scale=ns))
                P.op("act", lambda g, ns=nscol: g.activation(
                    dect()[:, 0:BLK], m1c[:], AF.Exp, bias=0.0, scale=ns))
                P.op("act", lambda g, ns=nscol: g.activation(
                    dect()[:, BLK:2 * BLK], m2c[:], AF.Exp, bias=0.0, scale=ns))
            P.op("act", lambda g, ns=nscol: g.activation(
                deck[:], a2c[:], AF.Exp, bias=0.0, scale=ns))
            P.op("act", lambda g, ns=nscol: g.activation(
                bdcol[:], c256c[:], AF.Exp, bias=0.0, scale=ns),
                incs=[(S.act, 1)])
            return S.act.n

        qkd_carry = [0]

        def scan_blocks(head, src_k, src_v, with_q, seed_from, store_seed,
                        act_kv_ready, o_dst=None, nblk=NBLK):
            iw = [(S.act, S.act.n)]
            if seed_from is None:
                P.op("dve", lambda g: g.memset(kvf[:], 0.0), waits=iw)
                P.op("dve", lambda g: g.memset(kvb[:, 0:DH], 0.0),
                     incs=[(S.dve, 1)])
            else:
                P.op("dve", lambda g, s=seed_from: g.tensor_copy(kvf[:], s),
                     waits=iw)
                P.op("dve", lambda g, s=seed_from: g.tensor_copy(
                    kvb[:, 0:DH], s), incs=[(S.dve, 1)])
            dv_kv = S.dve.n
            act_cast = S.act.n
            prev_tr_evac = 0
            prev_o_evac = {}
            prev_qkd = qkd_carry[0]
            for j in range(nblk):
                sl = j % 2
                t0 = j * BLK
                if "tr" in skip:
                    continue
                trw = [(S.act, act_kv_ready)]
                if prev_tr_evac:
                    trw = [(S.dve, prev_tr_evac[0]), (S.act, prev_tr_evac[1])]
                for ss in range(2):
                    P.op("pe", lambda g, s=ss, t=t0: g.transpose(
                        p_tr[:, s * 128:(s + 1) * 128],
                        src_k[:, t + s * 128: t + (s + 1) * 128], idb[:]),
                        waits=trw if ss == 0 else [])
                    trw = []
                for ss in range(2):
                    P.op("pe", lambda g, s=ss, t=t0: g.transpose(
                        p_trv[:, s * 128: (s + 1) * 128],
                        src_v[:, t + s * 128: t + (s + 1) * 128], idb[:]),
                        incs=[(S.pe, 1)] if ss == 1 else [])
                pe_tr = S.pe.n
                if "evk" in skip:
                    P.op("dve", lambda g: g.engine_nop(),
                         waits=[(S.pe, pe_tr)], incs=[(S.dve, 1)])
                else:
                    for ss in range(2):
                        P.op("dve", lambda g, s=ss, q=sl: g.tensor_scalar(
                            ktm(q)[:, s * 128:(s + 1) * 128],
                            p_tr[:, s * 128:(s + 1) * 128],
                            deck[:, s:s + 1], None, ALU.mult),
                            waits=[(S.pe, pe_tr)] if ss == 0 else [],
                            incs=[(S.dve, 1)] if ss == 1 else [])
                dv_ktm = S.dve.n
                if "evv" in skip:
                    P.op("act", lambda g: g.nop(),
                         waits=[(S.pe, pe_tr)], incs=[(S.act, 1)], no_auto=True)
                else:
                    for ss in range(2):
                        P.op("act", lambda g, s=ss, q=sl: g.copy(
                            vtok(q)[:, s * 128:(s + 1) * 128],
                            p_trv[:, s * 128: (s + 1) * 128]),
                            waits=[(S.pe, pe_tr)] if ss == 0 else [],
                            incs=[(S.act, 1)] if ss == 1 else [])
                act_vt = S.act.n
                prev_tr_evac = (dv_ktm, act_vt)

                if with_q:
                    P.op("dve", lambda g, q=sl, t=t0: g.tensor_mul(
                        qdb(q), qtile()[:, t:t + BLK], decq()),
                        incs=[(S.dve, 1)])
                    dv_qd = S.dve.n
                    qkw = []
                    if prev_qkd:
                        qkw.append((S.dve, prev_qkd))
                    for ss in range(2):
                        P.op("pe", lambda g, s=ss, t=t0: g.matmul(
                            p_qk[:, s * 256:(s + 1) * 256],
                            src_k[:, t + s * 128: t + (s + 1) * 128],
                            qtile()[:, t:t + BLK], start=True, stop=True),
                            waits=qkw if ss == 0 else [],
                            incs=[(S.pe, 1)] if ss == 1 else [])
                    pe_qk = S.pe.n
                    for ss in range(2):
                        P.op("dve", lambda g, s=ss, q=sl: g.tensor_tensor(
                            qkdb(q)[:, s * 256:(s + 1) * 256],
                            p_qk[:, s * 256:(s + 1) * 256],
                            dect()[:, s * 256:(s + 1) * 256], ALU.mult),
                            waits=[(S.pe, pe_qk)] if ss == 0 else [],
                            incs=[(S.dve, 1)] if ss == 1 else [])
                    prev_qkd = S.dve.n
                    qkd_carry[0] = S.dve.n
                    dv_qkd = S.dve.n
                    ow = [(S.dve, dv_qd), (S.act, act_cast)]
                    if j >= 2 and (j - 2) in prev_o_evac:
                        ow.append((S.act, prev_o_evac[j - 2]))
                    P.op("pe", lambda g, q=sl: g.matmul(
                        p_o(q), kvb[:, q * DH:(q + 1) * DH], qdb(q),
                        start=True, stop=False), waits=ow)
                    for ss in range(2):
                        P.op("pe", lambda g, s=ss, q=sl: g.matmul(
                            p_o(q), vtok(q)[:, s * 128:(s + 1) * 128],
                            qkdb(q)[:, s * 256:(s + 1) * 256],
                            start=False, stop=(s == 1)),
                            waits=[(S.dve, dv_qkd), (S.dve, dv_ktm),
                                   (S.act, act_vt)] if ss == 0 else [],
                            incs=[(S.pe, 1)] if ss == 1 else [])
                    pe_o = S.pe.n
                    P.op("act", lambda g, q=sl, t=t0, od=o_dst: g.copy(
                        od[:, t:t + BLK], p_o(q)),
                        waits=[(S.pe, pe_o)], incs=[(S.act, 1)])
                    prev_o_evac[j] = S.act.n
                if "amm" in skip:
                    continue
                aw = [(S.dve, dv_ktm), (S.act, act_vt), (S.dve, dv_kv)]
                for ss in range(2):
                    P.op("pe", lambda g, s=ss, q=sl: g.matmul(
                        p_A2, ktm(q)[:, s * 128:(s + 1) * 128],
                        vtok(q)[:, s * 128:(s + 1) * 128],
                        start=(s == 0), stop=(s == 1)),
                        waits=aw if ss == 0 else [],
                        incs=[(S.pe, 1)] if ss == 1 else [])
                pe_A = S.pe.n
                if "kvu" in skip:
                    continue
                P.op("dve", lambda g: g.drain(fusable=False))
                P.op("dve", lambda g: g.scalar_tensor_tensor(
                    kvf[:], kvf[:], bdcol[:, 0:1], p_A2, ALU.mult, ALU.add),
                    waits=[(S.pe, pe_A)], incs=[(S.dve, 1)])
                dv_kv = S.dve.n
                P.op("act", lambda g, q=(j + 1) % 2: g.copy(
                    kvb[:, q * DH:(q + 1) * DH], kvf[:]),
                    waits=[(S.dve, dv_kv)], incs=[(S.act, 1)])
                act_cast = S.act.n
            if store_seed is not None and "seed" not in skip:
                P.op("act", lambda g, d=store_seed: g.copy(d, kvf[:]),
                     waits=[(S.dve, dv_kv)], incs=[(S.act, 1)])
            return S.act.n

        prev_proj_pe = 0
        for head in range(a0_heads):
            slk = (head % 2) * 2
            wts = [(S.act, act_xnp)] if head == 0 else []
            if head >= 2:
                wts.append((S.pe, prev_proj_pe))
            wk = head_weights(head, [(1, slk), (2, slk + 1)], wts)
            dec_a = head_decays(head, need_q=False)
            pw = [wk[0], wk[1]]
            if head > 0:
                pw.append((S.act, S.act.n))
                pw.append((S.dve, S.dve.n))
            a_k, _ = project(ktile(), slk, abig.ap(), pw, sil=True)
            a_v, pe_v = project(vtile(), slk + 1, abig.ap(), [(S.act, a_k)],
                                sil=False)
            prev_proj_pe = pe_v
            scan_blocks(head, ktile(), vtile(), with_q=False,
                        seed_from=None,
                        store_seed=kvseed[:, head * DH:(head + 1) * DH],
                        act_kv_ready=a_v, nblk=a0_blocks)
        if "dbg_seed" in dbg_d:
            sdma(dbg_d["dbg_seed"].ap(), kvseed[:], waits=[(S.act, S.act.n)])

        barrier()

        if upto < 4:
            with nc.Block() as block:
                P.emit(block)
            return nc
        prev_proj_pe = 0
        for head in range(a_heads):
            slk = (head % 3)
            wts = []
            if head >= 1:
                wts.append((S.pe, prev_proj_pe))
            wq = head_weights(head, [(0, 0), (1, 1), (2, 2)], wts)
            dec_a = head_decays(head, need_q=True)
            pw = [wq[0], wq[1], wq[2]]
            if head > 0:
                pw.append((S.act, S.act.n))
                pw.append((S.dve, S.dve.n))
            a_q, _ = project(qtile(), 0, xn.ap(), pw, sil=True)
            a_k, _ = project(ktile(), 1, xn.ap(), [(S.act, a_q)], sil=True)
            a_v, pe_v = project(vtile(), 2, xn.ap(), [(S.act, a_k)], sil=False)
            prev_proj_pe = pe_v
            o_dst = abig.ap()[:, head * T:(head + 1) * T]
            scan_blocks(head, ktile(), vtile(), with_q=True,
                        seed_from=kvseed[:, head * DH:(head + 1) * DH],
                        store_seed=None, act_kv_ready=a_v, o_dst=o_dst)
        act_o_done = S.act.n
        if "dbg_o" in dbg_d:
            sdma(dbg_d["dbg_o"].ap(), abig.ap()[:], waits=[(S.act, act_o_done)])

        barrier()

        if upto < 5:
            with nc.Block() as block:
                P.emit(block)
            return nc
        dv_sq = {}
        for ht in range(HT):
            sl = ht % 2
            sq_w = []
            if ht >= 2:
                sq_w.append((S.pe, dv_sq[ht - 2]))
            P.op("dve", lambda g, h=ht, s=sl: g.tensor_mul(
                sqg(s), abig.ap()[:, h * T:(h + 1) * T],
                abig.ap()[:, h * T:(h + 1) * T]),
                waits=sq_w, incs=[(S.dve, 1)])
            dvn = S.dve.n
            for ch in range(4):
                P.op("pe", lambda g, c=ch, h=ht, s=sl: g.matmul(
                    p_r[:, c * 512:(c + 1) * 512], onesb[:],
                    sqg(s)[:, c * 512:(c + 1) * 512],
                    start=(h == 0), stop=(h == HT - 1)),
                    waits=[(S.dve, dvn)] if ch == 0 else [],
                    incs=[(S.pe, 1)] if ch == 3 else [])
            dv_sq[ht] = S.pe.n
        pe_r = S.pe.n
        for ch in range(4):
            P.op("dve", lambda g, c=ch: g.tensor_scalar(
                astr2.ap()[:, c * 512:(c + 1) * 512],
                p_r[:, c * 512:(c + 1) * 512], 1.0 / 2048.0, None, ALU.mult),
                waits=[(S.pe, pe_r)] if ch == 0 else [],
                incs=[(S.dve, 1)] if ch == 3 else [])
        dv_rt = S.dve.n
        P.op("act", lambda g: g.activation(
            astr.ap()[:], astr2.ap()[:], AF.Ln, bias=1e-6, scale=1.0),
            waits=[(S.dve, dv_rt)])
        P.op("act", lambda g: g.drain(fusable=False))
        P.op("act", lambda g: g.activation(
            rbv(), astr.ap()[:], AF.Exp, bias=0.0, scale=-0.5),
            incs=[(S.act, 1)])
        act_rb = S.act.n

        prev_upe = 0
        prev_act_u = 0
        dve_gu = {}
        for ht in range(HT):
            sl = ht % 2
            wts = [(S.act, act_rb)] if ht == 0 else []
            if ht >= 2:
                wts.append((S.pe, prev_upe))
            wu_ = wdma(uwg(sl), wuat.ap()[ht], waits=wts, sem=S_ug[sl])
            uw = [wu_]
            if prev_act_u:
                uw.append((S.act, prev_act_u))
            else:
                uw.append((S.act, act_rb))
            if ht >= 2:
                uw.append((S.dve, dve_gu[ht - 2]))
            for ch in range(4):
                for h2 in range(HT):
                    P.op("pe", lambda g, c=ch, h=h2, s=sl: g.matmul(
                        p_ug[:, c * 512:(c + 1) * 512],
                        uwg(s)[:, h * 128:(h + 1) * 128],
                        xn.ap()[:, h * T + c * 512: h * T + (c + 1) * 512],
                        start=(h == 0), stop=(h == HT - 1)),
                        waits=uw if (ch == 0 and h2 == 0) else [],
                        incs=[(S.pe, 1)] if (ch == 3 and h2 == HT - 1) else [])
            prev_upe = S.pe.n
            pe_u = S.pe.n
            P.op("act", lambda g, s=sl: g.copy(utg(s), p_ug[:]),
                 waits=[(S.pe, pe_u)], incs=[(S.act, 1)])
            prev_act_u = S.act.n
            act_u = S.act.n
            P.op("dve", lambda g, h=ht, s=sl: g.tensor_mul(
                gutv(s), abig.ap()[:, h * T:(h + 1) * T], rbv()),
                waits=[(S.act, act_u)])
            P.op("dve", lambda g: g.drain(fusable=False))
            P.op("dve", lambda g, h=ht, s=sl: g.tensor_mul(
                abig.ap()[:, h * T:(h + 1) * T], gutv(s), utg(s)),
                incs=[(S.dve, 1)])
            dve_gu[ht] = S.dve.n
        dv_gu_att = S.dve.n

        prev_spe = 0
        prev_out_dve = 0
        prev_store = 0
        for ot in range(HT):
            sl = ot % 2
            wts = [(S.dve, dv_gu_att)] if ot == 0 else []
            if ot >= 2:
                wts.append((S.pe, prev_spe))
            wo = wdma(woutg(sl), wout.ap()[ot], waits=wts, sem=S_wo[sl])
            wg_ = sdma(astr.ap()[:], xT.ap()[ot * 128:(ot + 1) * 128, :],
                       waits=[(S.dve, prev_out_dve)] if ot >= 1 else
                       [(S.act, act_rb)], sem=S.x)
            wb_ = gdma(gbk(), gspill.ap()[ot],
                       waits=[(S.dve, prev_out_dve)] if ot >= 1 else
                       [(S.sp, w_spill)], sem=S.gb)
            sw = [wo, (S.dve, dv_gu_att)]
            if ot >= 1:
                sw.append((S.dve, prev_out_dve))
            for ch in range(4):
                for h2 in range(HT):
                    P.op("pe", lambda g, c=ch, h=h2, s=sl: g.matmul(
                        p_s[:, c * 512:(c + 1) * 512],
                        woutg(s)[:, h * 128:(h + 1) * 128],
                        abig.ap()[:, h * T + c * 512: h * T + (c + 1) * 512],
                        start=(h == 0), stop=(h == HT - 1)),
                        waits=sw if (ch == 0 and h2 == 0) else [],
                        incs=[(S.pe, 1)] if (ch == 3 and h2 == HT - 1) else [])
            prev_spe = S.pe.n
            pe_s = S.pe.n
            dw = [(S.pe, pe_s), wb_, wg_]
            if ot == 0:
                dw.append((S.act, act_rb))
            else:
                dw.append((S.sp, prev_store))
            P.op("dve", lambda g: g.tensor_tensor(
                astr2.ap()[:], p_s[:], gbk(), ALU.mult), waits=dw)
            P.op("dve", lambda g: g.drain(fusable=False))
            P.op("dve", lambda g: g.tensor_add(
                astr2.ap()[:], astr2.ap()[:], astr.ap()[:]),
                incs=[(S.dve, 1)])
            prev_out_dve = S.dve.n
            prev_store = sdma(out_d.ap()[ot * 128:(ot + 1) * 128, :],
                              astr2.ap()[:], waits=[(S.dve, prev_out_dve)])
        w_out_done = S.sp.n
        P.op("sp", lambda g: g.nop(), waits=[(S.sp, w_out_done)])

        with nc.Block() as block:
            P.emit(block)

    return nc



def make_in_maps(L0_attn_x, L0_attn_slope_rate, L0_attn_W_qkvu, L0_attn_W_out,
                 L0_ffn_Wr, L0_ffn_Wg, L0_ffn_Wu, L0_ffn_Wd):
    x = np.asarray(L0_attn_x, dtype=np.float32)
    shared = stage_shared(L0_attn_W_qkvu, L0_attn_W_out, L0_ffn_Wr,
                          L0_ffn_Wg, L0_ffn_Wu, L0_ffn_Wd,
                          L0_attn_slope_rate)
    consts = stage_consts()
    zeros = np.zeros((D, T), np.float32)
    in_maps = []
    for c in range(8):
        b, half = c // 2, c % 2
        m = dict(shared)
        m.update(consts)
        m["xT"] = np.ascontiguousarray(x[b, half * T:(half + 1) * T, :].T)
        m["xTp"] = (np.ascontiguousarray(x[b, 0:T, :].T) if half == 1
                    else zeros)
        m["below"] = (np.arange(8) < c).astype(np.float32).reshape(8, 1)
        in_maps.append(m)
    return in_maps


def kernel(L0_attn_x, L0_attn_slope_rate, L0_attn_W_qkvu, L0_attn_W_out,
           L0_ffn_Wr, L0_ffn_Wg, L0_ffn_Wu, L0_ffn_Wd):
    in_maps = make_in_maps(L0_attn_x, L0_attn_slope_rate, L0_attn_W_qkvu,
                           L0_attn_W_out, L0_ffn_Wr, L0_ffn_Wg, L0_ffn_Wu,
                           L0_ffn_Wd)
    nc = build_nc()
    res = run_bass_kernel_spmd(nc, in_maps, core_ids=list(range(8)))
    out = np.empty((4, 4096, D), np.float32)
    for c in range(8):
        b, half = c // 2, c % 2
        out[b, half * T:(half + 1) * T, :] = res.results[c]["out"].T
    return out


# version 05
# speedup vs baseline: 1.0014x; 1.0014x over previous
import numpy as np
import ml_dtypes
import concourse.bass as bass
import concourse.mybir as mybir
from concourse.bass_utils import run_bass_kernel_spmd
from contextlib import ExitStack

F32 = mybir.dt.float32
BF16 = mybir.dt.bfloat16
AF = mybir.ActivationFunctionType
ALU = mybir.AluOpType
AX = mybir.AxisListType.X

D = 2048
T = 2048
NT = 16
HT = 16
NH = 16
DH = 128
BLK = 256
NBLK = 8
E = 8
NI = 11
TH = 1024
CAP = 5120
BIG = 1.0e9

ENGINES = ("gp", "pe", "act", "dve", "sp")


class Sem:
    def __init__(self, nc, ctx, name):
        self.h = ctx.enter_context(nc.semaphore(name))
        self.n = 0

    def bump(self, k=1):
        self.n += k
        return self.n


class Prog:

    def __init__(self):
        self.streams = {e: [] for e in ENGINES}
        self.auto = {}

    def op(self, eng, fn, waits=(), incs=(), no_auto=False):
        waits = list(waits)
        incs = list(incs)
        a = None if no_auto else self.auto.get(eng)
        if a is not None:
            if a.n > 0:
                waits.append((a, a.n))
            if not any(sem is a for sem, _ in incs):
                incs.append((a, 1))
        for sem, k in incs:
            sem.bump(k)
        self.streams[eng].append((waits, fn, incs))

    def emit(self, block):
        def make(engine_name):
            def body(eng):
                for waits, fn, incs in self.streams[engine_name]:
                    for sem, val in waits:
                        eng.wait_ge(sem.h, val)
                    instr = fn(eng)
                    for sem, k in incs:
                        if instr is None:
                            instr = eng.nop()
                        instr = instr.then_inc(sem.h, k)
            return body
        block.gpsimd(make("gp"))
        block.tensor(make("pe"))
        block.scalar(make("act"))
        block.vector(make("dve"))
        block.sync(make("sp"))


def _bf(a):
    return np.ascontiguousarray(a).astype(ml_dtypes.bfloat16)


def stage_shared(W_qkvu, W_out, Wr, Wg, Wu, Wd, slope):
    st = {}
    W4 = np.asarray(W_qkvu, dtype=np.float32).reshape(4, NH, DH, HT, 128)
    st["wqkv"] = np.ascontiguousarray(W4[0:3].transpose(1, 0, 3, 4, 2))
    st["wuat"] = np.ascontiguousarray(W4[3].transpose(0, 2, 3, 1))
    st["wr"] = np.ascontiguousarray(
        np.asarray(Wr, dtype=np.float32).T.reshape(HT, 128, E))
    A = np.asarray(Wg, dtype=np.float32).transpose(0, 2, 1)
    wg5 = A.reshape(E, HT, 128, NI, 128).transpose(0, 3, 1, 2, 4)
    A = np.asarray(Wu, dtype=np.float32).transpose(0, 2, 1)
    wu5 = A.reshape(E, HT, 128, NI, 128).transpose(0, 3, 1, 2, 4)
    st["mwgu"] = np.ascontiguousarray(np.stack([wg5, wu5], axis=2))
    A = np.asarray(Wd, dtype=np.float32).transpose(0, 2, 1)
    st["mwd"] = np.ascontiguousarray(
        A.reshape(E, NI, 128, HT, 128).transpose(0, 3, 1, 2, 4))
    A = np.asarray(W_out, dtype=np.float32).T
    st["wout"] = np.ascontiguousarray(
        A.reshape(HT, 128, HT, 128).transpose(2, 0, 1, 3))
    s = np.asarray(slope, dtype=np.float32).reshape(1, NH)
    st["negslope"] = np.ascontiguousarray(np.tile(-s, (128, 1)))
    return st


def stage_consts():
    st = {}
    st["tri128"] = np.ascontiguousarray(
        (np.arange(128)[:, None] <= np.arange(128)[None, :]).astype(np.float32))
    st["tri16x"] = np.ascontiguousarray(
        (np.arange(16)[:, None] < np.arange(16)[None, :]).astype(np.float32))
    st["ones128b"] = _bf(np.ones((128, 128), np.float32))
    st["id128b"] = _bf(np.eye(128, dtype=np.float32))
    st["onescol"] = np.ones((128, 1), np.float32)
    st["ones16col"] = np.ones((16, 1), np.float32)
    st["ones16row"] = np.ones((1, 16), np.float32)
    st["ones8col"] = np.ones((8, 1), np.float32)
    st["ones1row"] = np.ones((1, 128), np.float32)
    st["ones1rowb"] = _bf(np.ones((1, 128), np.float32))
    st["arange8"] = np.ascontiguousarray(
        np.tile(np.arange(8, dtype=np.float32)[None, :], (128, 1)))
    i1 = np.arange(1, BLK + 1, dtype=np.float32)
    st["a1row"] = _bf(np.tile(i1[None, :], (128, 1)))
    a2 = np.stack([BLK - i1[0:128], BLK - i1[128:256]], axis=1)
    st["a2cols"] = np.ascontiguousarray(a2.astype(np.float32))
    st["c256"] = np.full((128, 1), float(BLK), np.float32)
    s_idx = np.arange(BLK)[:, None]
    t_idx = np.arange(BLK)[None, :]
    M = np.where(t_idx >= s_idx, (t_idx - s_idx).astype(np.float32), BIG)
    st["m1d"] = _bf(M[0:128])
    st["m2d"] = _bf(M[128:256])
    st["cvals"] = np.ascontiguousarray(np.tile(np.array(
        [0.0, 1.0, -0.5, 1e-5, 1e-6, 1.0 / 2048.0], np.float32)[None, :],
        (128, 1)))
    st["zeros16"] = np.zeros((16, 128), np.float32)
    return st


CONST_VALS = [0.0, 1.0, -0.5, 1e-5, 1e-6, 1.0 / 2048.0]


def build_nc(debug=(), upto=5, m_experts=E, m_halves=2, a0_heads=NH, a_heads=NH, a0_blocks=NBLK, skip=()):
    nc = bass.Bass("TRN2", target_bir_lowering=False)
    dbg = set(debug)

    dram = {}
    def din(name, shape, dt):
        t = nc.dram_tensor(name, shape, dt, kind="ExternalInput")
        dram[name] = t
        return t

    xT = din("xT", [D, T], F32)
    xTp = din("xTp", [D, T], F32)
    wqkv = din("wqkv", [NH, 3, HT, 128, 128], F32)
    wuat = din("wuat", [HT, HT, 128, 128], F32)
    wr = din("wr", [HT, 128, E], F32)
    mwgu = din("mwgu", [E, NI, 2, HT, 128, 128], F32)
    mwd = din("mwd", [E, HT, NI, 128, 128], F32)
    wout = din("wout", [HT, HT, 128, 128], F32)
    negslope = din("negslope", [128, NH], F32)
    below = din("below", [8, 1], F32)
    tri128 = din("tri128", [128, 128], F32)
    tri16x = din("tri16x", [16, 16], F32)
    ones128b = din("ones128b", [128, 128], BF16)
    id128b = din("id128b", [128, 128], BF16)
    onescol = din("onescol", [128, 1], F32)
    ones16col = din("ones16col", [16, 1], F32)
    ones16row = din("ones16row", [1, 16], F32)
    ones8col = din("ones8col", [8, 1], F32)
    ones1row = din("ones1row", [1, 128], F32)
    ones1rowb = din("ones1rowb", [1, 128], BF16)
    arange8 = din("arange8", [128, 8], F32)
    a1row = din("a1row", [128, BLK], BF16)
    a2cols = din("a2cols", [128, 2], F32)
    c256 = din("c256", [128, 1], F32)
    m1d = din("m1d", [128, BLK], BF16)
    m2d = din("m2d", [128, BLK], BF16)
    cvals = din("cvals", [128, 6], F32)
    zeros16 = din("zeros16", [16, 128], F32)

    out_d = nc.dram_tensor("out", [D, T], F32, kind="ExternalOutput")
    gspill = nc.dram_tensor("gspill", [HT, 128, T], BF16, kind="Internal")
    rsc = nc.dram_tensor("rsc", [16, 16], F32, kind="Internal")
    ccin = nc.dram_tensor("ccin", [16, 128], F32, kind="Internal")
    ccout = nc.dram_tensor("ccout", [128, 128], F32, kind="Internal")

    dbg_d = {}
    for name, shape, dt in [
        ("dbg_xn", [128, HT * T], BF16), ("dbg_comb", [128, NT * 8], F32),
        ("dbg_y", [128, HT * T], BF16), ("dbg_o", [128, NH * T], BF16),
        ("dbg_logits", [128, NT * 8], F32), ("dbg_mr", [128, 2 * T], BF16),
        ("dbg_seed", [128, NH * 128], BF16), ("dbg_gath", [8, 16], F32),
        ("dbg_crow1", [1, 256], F32), ("dbg_gu", [128, NI * TH], BF16),
        ("dbg_w", [128, 2048], BF16), ("dbg_gt", [128, 2 * TH], BF16),
        ("dbg_cbB", [128, TH], BF16), ("dbg_y0", [128, HT * T], BF16),
    ]:
        if name in dbg:
            dbg_d[name] = nc.dram_tensor(name, shape, dt, kind="ExternalOutput")

    ctx = ExitStack()
    with ctx:
        sb = lambda n, s, d: ctx.enter_context(nc.sbuf_tensor(n, s, d))
        psal = lambda n, s, d: ctx.enter_context(nc.psum_tensor(n, s, d))

        xn = sb("xn", [128, HT * T], BF16)
        abig = sb("abig", [128, HT * T], BF16)
        agu = sb("agu", [128, NI * TH], BF16)
        wstr = sb("wstr", [128, 8192], BF16)
        astr = sb("astr", [128, T], F32)
        astr2 = sb("astr2", [128, T], F32)
        mr = sb("mr", [128, 2 * T], BF16)
        combT = sb("combT", [8, T], BF16)
        crow = sb("crow", [1, TH], BF16)
        combB = sb("combB", [128, TH], BF16)
        kvseed = sb("kvseed", [128, NH * DH], BF16)
        kvf = sb("kvf", [128, DH], F32)
        kvb = sb("kvb", [128, 2 * DH], BF16)
        deck = sb("deck", [128, 2], F32)
        bdcol = sb("bdcol", [128, 1], F32)
        tri = sb("tri", [128, 128], F32)
        trix = sb("trix", [16, 16], F32)
        onesb = sb("onesb", [128, 128], BF16)
        idb = sb("idb", [128, 128], BF16)
        onec = sb("onec", [128, 1], F32)
        one16 = sb("one16", [16, 1], F32)
        one16r = sb("one16r", [1, 16], F32)
        one8 = sb("one8", [8, 1], F32)
        one1r = sb("one1r", [1, 128], F32)
        one1rb = sb("one1rb", [1, 128], BF16)
        ar8 = sb("ar8", [128, 8], F32)
        a1r = sb("a1r", [128, BLK], BF16)
        a2c = sb("a2c", [128, 2], F32)
        c256c = sb("c256c", [128, 1], F32)
        m1c = sb("m1c", [128, BLK], BF16)
        m2c = sb("m2c", [128, BLK], BF16)
        cv = sb("cv", [128, 6], F32)
        wrc = sb("wrc", [128, HT * E], F32)
        nsl = sb("nsl", [128, NH], F32)
        belowc = sb("belowc", [8, 1], F32)

        for i, v in enumerate(CONST_VALS):
            nc.const_aps.register(F32, v, cv[:, i:i + 1])

        abf = abig.ap().bitcast(F32)
        asb = astr.ap().bitcast(BF16)
        asb2 = astr2.ap().bitcast(BF16)
        logit = lambda: abf[:, 0:128]
        gatet = lambda: abf[:, 128:256]
        rtmpa = lambda: abf[:, 256:384]
        rtmpb = lambda: abf[:, 384:512]
        rcol = lambda: abf[:, 512:576]
        ind12 = lambda: abf[:, 576:832]
        compsb = lambda: abf[:, 832:960]
        cntrow = lambda: abf[0:1, 960:1216]
        cnts16 = lambda: abf[0:16, 1216:1232]
        gath = lambda: abf[0:8, 1232:1248]
        offtot = lambda: abf[0:1, 1248:1280]
        baserow = lambda: abf[0:1, 1280:1296]
        ccar = lambda: abf[0:16, 1296:1312]
        crow1 = lambda: abf[0:1, 1312:1568]
        sstk = lambda: abf[:, 1600:1728]
        compsbB = lambda: abf.bitcast(BF16)[:, 3456:3584]
        agv = agu.ap()
        qtile = lambda: agv[:, 0:2048]
        ktile = lambda: agv[:, 2048:4096]
        vtile = lambda: agv[:, 4096:6144]
        ktm = lambda s: agv[:, 6144 + s * 256: 6144 + (s + 1) * 256]
        vtok = lambda s: agv[:, 6656 + s * 256: 6656 + (s + 1) * 256]
        qdb = lambda s: agv[:, 7168 + s * 256: 7168 + (s + 1) * 256]
        qkdb = lambda s: agv[:, 7680 + s * 512: 7680 + (s + 1) * 512]
        decq = lambda: agv[:, 8704:8960]
        dect = lambda: agv[:, 8960:9472]
        sqg = lambda s: agv[:, s * 2048:(s + 1) * 2048]
        utg = lambda s: agv[:, 4096 + s * 2048: 4096 + (s + 1) * 2048]
        gutg = lambda s: agv[:, 8704 + (s % 1) * 0: 0]
        wsv = wstr.ap()
        wq_slot = lambda s: wsv[:, s * 2048:(s + 1) * 2048]
        woutg = lambda s: wsv[:, s * 2048:(s + 1) * 2048]
        uwg = lambda s: wsv[:, (2 + s) * 2048:(3 + s) * 2048]
        def wgu_sl(i):
            return (asb[:, i * 2048:(i + 1) * 2048] if i < 2
                    else asb2[:, (i - 2) * 2048:(i - 1) * 2048])
        wd_sl = lambda s: wsv[:, s * 1408:(s + 1) * 1408]
        gutv = lambda s: agv[:, s * 2048:(s + 1) * 2048]
        rbv = lambda: mr.ap()[:, 0:T]
        gbk = lambda: mr.ap()[:, T:2 * T]

        pA = psal("pA", [128, 2048], F32)
        pB = psal("pB", [128, 2048], F32)
        pAb = pA.ap().bitcast(BF16)
        pBb = pB.ap().bitcast(BF16)
        p_m = pA.ap()
        p_s2 = pB.ap()
        p_log = lambda s: pA.ap()[:, s * 512: s * 512 + 128]
        p_rank = pA.ap()[:, 128:144]
        p_sm = pA.ap()[0:1, 512:1024]
        p_c16 = pA.ap()[0:16, 1024:1040]
        p_ct = pB.ap()[0:8, :]
        p_cbB = pA.ap()[:, 0:1024]
        p_g = pA.ap()[:, 0:1024]
        p_u = pA.ap()[:, 1024:2048]
        p_y = lambda s: pB.ap()[:, s * 1024:(s + 1) * 1024]
        p_proj = pA.ap()
        p_tr = pAb[:, 0:512]
        p_trv = pAb[:, 1024:1536]
        p_A2 = pA.ap()[:, 1536:1664]
        p_o = lambda s: pB.ap()[:, s * 512:(s + 1) * 512][:, 0:256]
        p_qk = pB.ap()[:, 1024:1536]
        p_r = pA.ap()
        p_ug = pB.ap()
        p_s = pA.ap()

        sems = {}
        for nm in ["gp", "sp", "pe", "act", "dve", "cc", "bar", "st",
                   "xb0", "xb1", "x", "wg0", "wg1", "wu0", "wu1",
                   "wd0", "wd1", "cr", "wq0", "wq1", "wq2", "wq3",
                   "ug0", "ug1", "wo0", "wo1", "gb"]:
            sems[nm] = Sem(nc, ctx, "s_" + nm)
        S = type("S", (), sems)
        S_xb = [S.xb0, S.xb1]
        S_wg = [S.wg0, S.wg1]
        S_wu = [S.wu0, S.wu1]
        S_wd = [S.wd0, S.wd1]
        S_wq = [S.wq0, S.wq1, S.wq2, S.wq3]
        S_ug = [S.ug0, S.ug1]
        S_wo = [S.wo0, S.wo1]
        P = Prog()
        P.auto["dve"] = S.dve
        P.auto["act"] = S.act

        def barrier():
            for e in ENGINES:
                P.op(e, lambda g: g.drain(fusable=False))
                P.op(e, lambda g: g.nop(), incs=[(S.bar, 1)], no_auto=True)
            tgt = S.bar.n
            for e in ENGINES:
                P.op(e, lambda g: g.nop(), waits=[(S.bar, tgt)], no_auto=True)

        def wdma(dst, src, waits=(), sem=None):
            sem = sem or S.gp
            d = dst.rearrange("p (h c) -> p h c", c=128)
            sr = src.rearrange("h p c -> p h c")
            P.op("gp", lambda g, d=d, sr=sr: g.dma_start(d, sr),
                 waits=waits, incs=[(sem, 16)])
            return (sem, sem.n)

        def gdma(dst, src, waits=(), sem=None):
            sem = sem or S.gp
            P.op("gp", lambda g, d=dst, s=src: g.dma_start(d, s),
                 waits=waits, incs=[(sem, 16)])
            return (sem, sem.n) if sem is not S.gp else sem.n

        def sdma(dst, src, waits=(), sem=None):
            sem = sem or S.sp
            P.op("sp", lambda g, d=dst, s=src: g.dma_start(d, s),
                 waits=waits, incs=[(sem, 16)])
            return (sem, sem.n) if sem is not S.sp else sem.n

        wl = 0
        for dst, src in [
            (tri[:], tri128.ap()), (trix[:], tri16x.ap()),
            (onesb[:], ones128b.ap()), (idb[:], id128b.ap()),
            (onec[:], onescol.ap()), (one16[:], ones16col.ap()),
            (one16r[:], ones16row.ap()),
            (one8[:], ones8col.ap()), (one1r[:], ones1row.ap()),
            (one1rb[:], ones1rowb.ap()),
            (ar8[:], arange8.ap()), (a1r[:], a1row.ap()), (a2c[:], a2cols.ap()),
            (c256c[:], c256.ap()), (m1c[:], m1d.ap()), (m2c[:], m2d.ap()),
            (cv[:], cvals.ap()),
            (wrc[:].rearrange("p (h e) -> p h e", e=E),
             wr.ap().rearrange("h p e -> p h e")),
            (nsl[:], negslope.ap()), (belowc[:], below.ap()),
        ]:
            wl = sdma(dst, src)
        w_const = wl
        wz_ccin = gdma(ccin.ap(), zeros16.ap())
        for e in ENGINES:
            P.op(e, lambda g: g.nop(), waits=[(S.sp, w_const)])

        def ln_phase(x_dram, xn_arena, do_router):
            pe_done = []
            for ht in range(HT):
                sl = ht % 2
                xb = asb[:, sl * 2048:(sl + 1) * 2048]
                x2 = asb2[:, sl * 2048:(sl + 1) * 2048]
                waits = [(S.pe, pe_done[ht - 2])] if ht >= 2 else []
                w = gdma(xb, x_dram.ap()[ht * 128:(ht + 1) * 128, :],
                         waits=waits, sem=S_xb[sl])
                P.op("dve", lambda g, xb=xb, x2=x2: g.tensor_mul(x2, xb, xb),
                     waits=[w], incs=[(S.dve, 1)])
                dvn = S.dve.n
                for ch in range(4):
                    P.op("pe", lambda g, xb=xb, c=ch, h=ht: g.matmul(
                        p_m[:, c * 512:(c + 1) * 512], onesb[:],
                        xb[:, c * 512:(c + 1) * 512],
                        start=(h == 0), stop=(h == HT - 1)),
                        waits=[w] if ch == 0 else [])
                for ch in range(4):
                    P.op("pe", lambda g, x2=x2, c=ch, h=ht: g.matmul(
                        p_s2[:, c * 512:(c + 1) * 512], onesb[:],
                        x2[:, c * 512:(c + 1) * 512],
                        start=(h == 0), stop=(h == HT - 1)),
                        waits=[(S.dve, dvn)] if ch == 0 else [],
                        incs=[(S.pe, 1)] if ch == 3 else [])
                pe_done.append(S.pe.n)
            pe_stats = S.pe.n
            for ch in range(4):
                P.op("dve", lambda g, c=ch: g.tensor_scalar(
                    astr2.ap()[:, c * 512:(c + 1) * 512],
                    p_s2[:, c * 512:(c + 1) * 512],
                    1.0 / 2048.0, None, ALU.mult),
                    waits=[(S.pe, pe_stats)] if ch == 0 else [])
            for ch in range(4):
                P.op("dve", lambda g, c=ch: g.tensor_scalar(
                    astr.ap()[:, c * 512:(c + 1) * 512],
                    p_m[:, c * 512:(c + 1) * 512],
                    1.0 / 2048.0, None, ALU.mult))
            P.op("dve", lambda g: g.drain(fusable=False))
            for ch in range(4):
                P.op("dve", lambda g, c=ch: g.tensor_copy(
                    mr.ap()[:, c * 512:(c + 1) * 512],
                    astr.ap()[:, c * 512:(c + 1) * 512]))
            for ch in range(4):
                P.op("dve", lambda g, c=ch: g.tensor_mul(
                    abf[:, 2048 + c * 512: 2048 + (c + 1) * 512],
                    astr.ap()[:, c * 512:(c + 1) * 512],
                    astr.ap()[:, c * 512:(c + 1) * 512]))
            for ch in range(4):
                P.op("dve", lambda g, c=ch: g.tensor_sub(
                    astr2.ap()[:, c * 512:(c + 1) * 512],
                    astr2.ap()[:, c * 512:(c + 1) * 512],
                    abf[:, 2048 + c * 512: 2048 + (c + 1) * 512]),
                    incs=[(S.dve, 1)] if ch == 3 else [])
            dv_var = S.dve.n
            P.op("act", lambda g: g.activation(
                abf[:, 2048:4096].bitcast(F32) if False else abf[:, 2048:4096],
                astr2.ap()[:], AF.Ln, bias=1e-5, scale=1.0),
                waits=[(S.dve, dv_var)])
            P.op("act", lambda g: g.drain(fusable=False))
            P.op("act", lambda g: g.activation(
                mr.ap()[:, T:2 * T], abf[:, 2048:4096], AF.Exp,
                bias=0.0, scale=-0.5), incs=[(S.act, 1)])
            act_r = S.act.n
            pe_ht = []
            dv_lacc = {}
            for ht in range(HT):
                waits = [(S.act, act_r)]
                if ht >= 1:
                    waits = [(S.act, act_r + ht)]
                    if do_router:
                        waits.append((S.pe, pe_ht[ht - 1]))
                w = sdma(astr.ap()[:], x_dram.ap()[ht * 128:(ht + 1) * 128, :],
                         waits=waits, sem=S.x)
                P.op("dve", lambda g: g.tensor_sub(
                    astr2.ap()[:], astr.ap()[:], mr.ap()[:, 0:T]),
                    waits=[w, (S.act, act_r)])
                P.op("dve", lambda g: g.drain(fusable=False))
                P.op("dve", lambda g: g.tensor_mul(
                    astr.ap()[:], astr2.ap()[:], mr.ap()[:, T:2 * T]),
                    incs=[(S.dve, 1)])
                dv_xnf = S.dve.n
                P.op("act", lambda g, h=ht: g.copy(
                    xn_arena[:, h * T:(h + 1) * T], astr.ap()[:]),
                    waits=[(S.dve, dv_xnf)], incs=[(S.act, 1)])
                if do_router:
                    lsl = ht % 2
                    pw = [(S.dve, dv_xnf)]
                    if ht >= 2:
                        pw.append((S.dve, dv_lacc[ht - 2]))
                    for tt in range(NT):
                        P.op("pe", lambda g, h=ht, t=tt, q=lsl: g.matmul(
                            p_log(q)[:, t * E:(t + 1) * E],
                            astr.ap()[:, t * 128:(t + 1) * 128],
                            wrc[:, h * E:(h + 1) * E],
                            start=True, stop=True),
                            waits=pw if tt == 0 else [],
                            incs=[(S.pe, 1)] if tt == NT - 1 else [])
                    pe_l = S.pe.n
                    if ht == 0:
                        P.op("dve", lambda g, q=lsl: g.tensor_copy(
                            logit(), p_log(q)),
                            waits=[(S.pe, pe_l)], incs=[(S.dve, 1)])
                    else:
                        P.op("dve", lambda g, q=lsl: g.tensor_tensor(
                            logit(), logit(), p_log(q), ALU.add),
                            waits=[(S.pe, pe_l)], incs=[(S.dve, 1)])
                    dv_lacc[ht] = S.dve.n
                pe_ht.append(S.pe.n)
            return S.act.n

        act_xn = ln_phase(xT, xn.ap(), do_router=True)
        pe_logits = S.pe.n
        if "dbg_mr" in dbg_d:
            sdma(dbg_d["dbg_mr"].ap(), mr.ap()[:], waits=[(S.act, act_xn)])

        P.op("act", lambda g: g.nop(), waits=[(S.dve, S.dve.n)],
             incs=[(S.act, 1)])
        act_log = S.act.n
        LT = lambda tt: logit()[:, tt * E:(tt + 1) * E]
        GT = lambda tt: gatet()[:, tt * E:(tt + 1) * E]
        TA = lambda tt: rtmpa()[:, tt * E:(tt + 1) * E]
        TB = lambda tt: rtmpb()[:, tt * E:(tt + 1) * E]
        I1 = lambda tt: ind12()[:, tt * 16: tt * 16 + 8]
        I2 = lambda tt: ind12()[:, tt * 16 + 8: tt * 16 + 16]
        CM1 = lambda tt: rcol()[:, tt:tt + 1]
        CM2 = lambda tt: rcol()[:, NT + tt:NT + tt + 1]
        CI1 = lambda tt: rcol()[:, 2 * NT + tt:2 * NT + tt + 1]
        CI2 = lambda tt: rcol()[:, 3 * NT + tt:3 * NT + tt + 1]

        def dve_all(fn, waits_first=(), inc_last=True):
            for tt in range(NT):
                P.op("dve", lambda g, t=tt: fn(g, t),
                     waits=list(waits_first) if tt == 0 else [],
                     incs=[(S.dve, 1)] if (inc_last and tt == NT - 1) else [])
            return S.dve.n

        dve_all(lambda g, t: g.tensor_reduce(CM1(t), LT(t), AX, ALU.max,
                                             negate=True),
                waits_first=[(S.act, act_log)])
        dv1 = S.dve.n
        for tt in range(NT):
            P.op("act", lambda g, t=tt: g.activation(
                GT(t), LT(t), AF.Exp, bias=CM1(t), scale=1.0),
                waits=[(S.dve, dv1)] if tt == 0 else [],
                incs=[(S.act, 1)] if tt == NT - 1 else [])
        act2 = S.act.n
        dve_all(lambda g, t: g.tensor_reduce(CM2(t), GT(t), AX, ALU.add),
                waits_first=[(S.act, act2)], inc_last=False)
        dve_all(lambda g, t: g.reciprocal(CM2(t), CM2(t)), inc_last=False)
        dve_all(lambda g, t: g.tensor_scalar(GT(t), GT(t), CM2(t), None,
                                             ALU.mult), inc_last=False)
        dve_all(lambda g, t: g.tensor_reduce(CM1(t), GT(t), AX, ALU.max),
                inc_last=False)
        dve_all(lambda g, t: g.tensor_scalar(TA(t), GT(t), CM1(t), None,
                                             ALU.is_equal), inc_last=False)
        dve_all(lambda g, t: g.tensor_mul(TB(t), TA(t), ar8[:]), inc_last=False)
        dve_all(lambda g, t: g.tensor_scalar(TA(t), TA(t), -8.0, 8.0,
                                             ALU.mult, ALU.add), inc_last=False)
        dve_all(lambda g, t: g.tensor_add(TB(t), TB(t), TA(t)), inc_last=False)
        dve_all(lambda g, t: g.tensor_reduce(CI1(t), TB(t), AX, ALU.min),
                inc_last=False)
        dve_all(lambda g, t: g.tensor_scalar(I1(t), ar8[:], CI1(t), None,
                                             ALU.is_equal), inc_last=False)
        dve_all(lambda g, t: g.scalar_tensor_tensor(TA(t), I1(t), -1.0e30,
                                                    GT(t), ALU.mult, ALU.add),
                inc_last=False)
        dve_all(lambda g, t: g.tensor_reduce(CM2(t), TA(t), AX, ALU.max),
                inc_last=False)
        dve_all(lambda g, t: g.tensor_scalar(TA(t), TA(t), CM2(t), None,
                                             ALU.is_equal), inc_last=False)
        dve_all(lambda g, t: g.tensor_mul(TB(t), TA(t), ar8[:]), inc_last=False)
        dve_all(lambda g, t: g.tensor_scalar(TA(t), TA(t), -8.0, 8.0,
                                             ALU.mult, ALU.add), inc_last=False)
        dve_all(lambda g, t: g.tensor_add(TB(t), TB(t), TA(t)), inc_last=False)
        dve_all(lambda g, t: g.tensor_reduce(CI2(t), TB(t), AX, ALU.min),
                inc_last=False)
        dve_all(lambda g, t: g.tensor_scalar(I2(t), ar8[:], CI2(t), None,
                                             ALU.is_equal))
        dv_ind = S.dve.n

        for tt in range(NT):
            P.op("pe", lambda g, t=tt: g.matmul(
                p_sm[0:1, t * 16: t * 16 + 8], onec[:], I1(t),
                start=True, stop=True),
                waits=[(S.dve, dv_ind)] if tt == 0 else [])
            P.op("pe", lambda g, t=tt: g.matmul(
                p_sm[0:1, t * 16 + 8: t * 16 + 16], onec[:], I2(t),
                start=True, stop=True),
                incs=[(S.pe, 1)] if tt == NT - 1 else [])
        pe_cnt = S.pe.n
        P.op("act", lambda g: g.copy(cntrow(), p_sm[0:1, 0:256]),
             waits=[(S.pe, pe_cnt)], incs=[(S.act, 1)])
        act_cnt = S.act.n
        w1 = gdma(rsc.ap().rearrange("a b -> (a b)").rearrange("(o x) -> o x", o=1), cntrow(),
                  waits=[(S.act, act_cnt)])
        w2 = gdma(cnts16(), rsc.ap(), waits=[(S.gp, w1)])
        P.op("pe", lambda g: g.matmul(p_sm[0:1, 256:272], one16[:], cnts16(),
                                      start=True, stop=True),
             waits=[(S.gp, w2)], incs=[(S.pe, 1)])
        pe_tot = S.pe.n
        P.op("act", lambda g: g.copy(cntrow()[0:1, 0:16], p_sm[0:1, 256:272]),
             waits=[(S.pe, pe_tot)], incs=[(S.act, 1)])
        act_tot = S.act.n
        w3 = gdma(ccin.ap()[0:1, 0:16], cntrow()[0:1, 0:16],
                  waits=[(S.act, act_tot), (S.gp, w2)])
        P.op("gp", lambda g: g.collective_compute(
            "AllGather", ALU.bypass,
            replica_groups=[[0, 1, 2, 3, 4, 5, 6, 7]],
            ins=[ccin.ap().opt()], outs=[ccout.ap().opt()]),
            waits=[(S.gp, w3)], incs=[(S.cc, 1)])
        w4 = gdma(gath(), ccout.ap().rearrange("(c r) f -> c r f", r=16)[:, 0, 0:16],
                  waits=[(S.cc, 1)])
        if "dbg_gath" in dbg_d:
            sdma(dbg_d["dbg_gath"].ap(), gath(), waits=[(S.gp, w4)])
        P.op("pe", lambda g: g.matmul(p_sm[0:1, 272:288], belowc[:], gath(),
                                      start=True, stop=True),
             waits=[(S.gp, w4)])
        P.op("pe", lambda g: g.matmul(p_sm[0:1, 288:304], one8[:], gath(),
                                      start=True, stop=True),
             incs=[(S.pe, 1)])
        pe_off = S.pe.n
        P.op("act", lambda g: g.copy(offtot(), p_sm[0:1, 272:304]),
             waits=[(S.pe, pe_off)], incs=[(S.act, 1)])
        act_off = S.act.n
        P.op("dve", lambda g: g.tensor_copy(baserow()[0:1, 0:8],
                                            offtot()[0:1, 0:8]),
             waits=[(S.act, act_off)])
        P.op("dve", lambda g: g.tensor_add(
            baserow()[0:1, 8:16], offtot()[0:1, 8:16], offtot()[0:1, 16:24]),
            incs=[(S.dve, 1)])
        dv_base = S.dve.n
        P.op("pe", lambda g: g.matmul(p_c16, trix[:], cnts16(),
                                      start=True, stop=False),
             waits=[(S.dve, dv_base)])
        P.op("pe", lambda g: g.matmul(p_c16, one16r[:], baserow(),
                                      start=False, stop=True),
             incs=[(S.pe, 1)])
        pe_car = S.pe.n
        P.op("act", lambda g: g.copy(ccar(), p_c16),
             waits=[(S.pe, pe_car)], incs=[(S.act, 1)])
        act_car = S.act.n
        w5 = gdma(rsc.ap(), ccar(), waits=[(S.act, act_car)])
        w6 = gdma(crow1(), rsc.ap().rearrange("a b -> (a b)").rearrange("(o x) -> o x", o=1),
                  waits=[(S.gp, w5)])
        if "dbg_crow1" in dbg_d:
            sdma(dbg_d["dbg_crow1"].ap(), crow1(), waits=[(S.gp, w6)])

        p_rk = lambda t: pA.ap()[:, 256 + t * 16: 256 + (t + 1) * 16]
        KM = lambda t: abf[:, 2048 + t * 16: 2048 + (t + 1) * 16]
        KI = lambda t: abf[:, 2304 + t * 16: 2304 + (t + 1) * 16]
        KF = lambda t: abf[:, 2560 + 2 * t: 2560 + 2 * t + 2]
        WK = lambda t: abf[:, 2592 + 2 * t: 2592 + 2 * t + 2]
        C1 = lambda t: abf[:, 2624 + t * 8: 2624 + (t + 1) * 8]
        C2 = lambda t: abf[:, 2752 + t * 8: 2752 + (t + 1) * 8]
        for tt in range(NT):
            P.op("pe", lambda g, t=tt: g.matmul(
                p_rk(t), tri[:], ind12()[:, t * 16:(t + 1) * 16],
                start=True, stop=False),
                waits=[(S.gp, w6), (S.dve, dv_ind)] if tt == 0 else [])
            P.op("pe", lambda g, t=tt: g.matmul(
                p_rk(t), one1r[:], crow1()[0:1, t * 16:(t + 1) * 16],
                start=False, stop=True),
                incs=[(S.pe, 1)] if tt == NT - 1 else [])
        pe_rank = S.pe.n
        dve_all(lambda g, t: g.tensor_scalar(
            KM(t), p_rk(t), float(CAP), None, ALU.is_le),
            waits_first=[(S.pe, pe_rank)], inc_last=False)
        dve_all(lambda g, t: g.tensor_mul(
            KI(t), KM(t), ind12()[:, t * 16:(t + 1) * 16]), inc_last=False)
        dve_all(lambda g, t: g.tensor_reduce(
            KF(t)[:, 0:1], KI(t)[:, 0:8], AX, ALU.add), inc_last=False)
        dve_all(lambda g, t: g.tensor_reduce(
            KF(t)[:, 1:2], KI(t)[:, 8:16], AX, ALU.add), inc_last=False)
        dve_all(lambda g, t: g.tensor_mul(
            WK(t)[:, 0:1], CM1(t), KF(t)[:, 0:1]), inc_last=False)
        dve_all(lambda g, t: g.tensor_mul(
            WK(t)[:, 1:2], CM2(t), KF(t)[:, 1:2]), inc_last=False)
        dve_all(lambda g, t: g.tensor_scalar(
            C1(t), I1(t), WK(t)[:, 0:1], None, ALU.mult), inc_last=False)
        dve_all(lambda g, t: g.tensor_scalar(
            C2(t), I2(t), WK(t)[:, 1:2], None, ALU.mult), inc_last=False)
        dve_all(lambda g, t: g.tensor_add(
            compsb()[:, t * 8:(t + 1) * 8], C1(t), C2(t)))
        dv_comb = S.dve.n
        if "dbg_comb" in dbg_d:
            sdma(dbg_d["dbg_comb"].ap(), compsb(), waits=[(S.dve, dv_comb)])
        if "dbg_logits" in dbg_d:
            sdma(dbg_d["dbg_logits"].ap(), logit(), waits=[(S.dve, dv_comb)])
        if "dbg_xn" in dbg_d:
            sdma(dbg_d["dbg_xn"].ap(), xn.ap()[:], waits=[(S.act, act_xn)])
        P.op("dve", lambda g: g.tensor_copy(compsbB(), compsb()),
             waits=[(S.dve, dv_comb)], incs=[(S.dve, 1)])
        dv_cb = S.dve.n
        for tt in range(NT):
            P.op("pe", lambda g, t=tt: g.transpose(
                pBb[0:8, t * 128:(t + 1) * 128],
                compsbB()[:, t * 8:(t + 1) * 8], idb[:]),
                waits=[(S.dve, dv_cb)] if tt == 0 else [],
                incs=[(S.pe, 1)] if tt == NT - 1 else [])
        pe_ct = S.pe.n
        P.op("act", lambda g: g.copy(combT[:], pBb[0:8, 0:2048]),
             waits=[(S.pe, pe_ct)], incs=[(S.act, 1)])
        act_ct = S.act.n

        barrier()

        if upto < 2:
            with nc.Block() as block:
                P.emit(block)
            return nc
        pe_after_gu = {}
        gu_slot_free = {}
        wd_slot_free = {}
        y_evac = {}
        last_y_dve = 0
        act_gu_evac = {}
        crow_w = {}
        prev_pe_cbB = 0
        prev_act_cbB = 0


        gu_dma = {}
        wd_dma = {}
        dv_prev_gu_all = 0

        for hh in range(m_halves):
            tof = hh * TH
            for e in range(m_experts):
                wcr = gdma(crow[0:1, 0:TH],
                           combT[e:e + 1, tof:tof + TH],
                           waits=[(S.act, act_ct)] if (hh == 0 and e == 0)
                           else [(S.dve, last_y_dve)], sem=S.cr)
                cb_waits = [wcr]
                if prev_act_cbB:
                    cb_waits.append((S.act, S.act.n))
                for ch in range(2):
                    P.op("pe", lambda g, c=ch: g.matmul(
                        p_cbB[:, c * 512:(c + 1) * 512], one1rb[:],
                        crow[0:1, c * 512:(c + 1) * 512],
                        start=True, stop=True),
                        waits=cb_waits if ch == 0 else [],
                        incs=[(S.pe, 1)] if ch == 1 else [])
                pe_cbB = S.pe.n
                P.op("act", lambda g: g.copy(combB[:], p_cbB),
                     waits=[(S.pe, pe_cbB)], incs=[(S.act, 1)])
                prev_act_cbB = S.act.n
                act_cbB = S.act.n

                for it in range(NI):
                    slot = it % 2
                    key = (hh, e, it)
                    wts = []
                    if slot in pe_after_gu:
                        wts.append((S.pe, pe_after_gu[slot]))
                    wg = wdma(wgu_sl(slot * 2), mwgu.ap()[e, it, 0],
                              waits=wts, sem=S_wg[slot])
                    if hh == 0 and e == 0 and it == 0 and "dbg_w" in dbg_d:
                        sdma(dbg_d["dbg_w"].ap(), wgu_sl(0), waits=[wg])
                    wu = wdma(wgu_sl(slot * 2 + 1), mwgu.ap()[e, it, 1],
                              sem=S_wu[slot])
                    gw = [wg, (S.act, act_cbB)]
                    if (hh, e, it - 1) in act_gu_evac:
                        gw.append((S.act, act_gu_evac[(hh, e, it - 1)]))
                    for ch in range(2):
                        for ht in range(HT):
                            P.op("pe", lambda g, c=ch, h=ht, s=slot, o=tof:
                                 g.matmul(
                                p_g[:, c * 512:(c + 1) * 512],
                                wgu_sl(s * 2)[:, h * 128:(h + 1) * 128],
                                xn.ap()[:, h * T + o + c * 512:
                                        h * T + o + (c + 1) * 512],
                                start=(h == 0), stop=(h == HT - 1)),
                                waits=gw if (ch == 0 and ht == 0) else [])
                    for ch in range(2):
                        for ht in range(HT):
                            P.op("pe", lambda g, c=ch, h=ht, s=slot, o=tof:
                                 g.matmul(
                                p_u[:, c * 512:(c + 1) * 512],
                                wgu_sl(s * 2 + 1)[:, h * 128:(h + 1) * 128],
                                xn.ap()[:, h * T + o + c * 512:
                                        h * T + o + (c + 1) * 512],
                                start=(h == 0), stop=(h == HT - 1)),
                                waits=[wu] if (ch == 0 and ht == 0)
                                else [],
                                incs=[(S.pe, 1)] if (ch == 1 and ht == HT - 1)
                                else [])
                    pe_after_gu[slot] = S.pe.n
                    pe_gu = S.pe.n
                    gsl = it % 2
                    gtmp = wsv[:, (4 + gsl) * 1024:(5 + gsl) * 1024]
                    utmp = wsv[:, (6 + gsl) * 1024:(7 + gsl) * 1024]
                    aw2 = [(S.pe, pe_gu)]
                    if (hh, e, it - 2) in gu_dma:
                        aw2.append((S.dve, gu_dma[(hh, e, it - 2)]))
                    elif it < 2 and (e > 0 or hh > 0):
                        aw2.append((S.dve, dv_prev_gu_all))
                    P.op("act", lambda g, gt=gtmp: g.activation(
                        gt, p_g[:, 0:1024], AF.Silu, bias=0.0, scale=1.0),
                        waits=aw2)
                    P.op("act", lambda g, ut=utmp: g.copy(ut, p_u[:, 0:1024]),
                         incs=[(S.act, 1)])
                    act_gu_evac[key] = S.act.n
                    a_gu = S.act.n
                    if hh == 0 and e == 0 and it == 0 and "dbg_gt" in dbg_d:
                        sdma(dbg_d["dbg_gt"].ap()[:, 0:TH], gtmp,
                             waits=[(S.act, a_gu)])
                        sdma(dbg_d["dbg_gt"].ap()[:, TH:2 * TH], utmp,
                             waits=[(S.act, a_gu)])
                    P.op("dve", lambda g, gt=gtmp, ut=utmp, i=it: g.tensor_mul(
                        agv[:, i * TH:(i + 1) * TH], gt, ut),
                        waits=[(S.act, a_gu)])
                    P.op("dve", lambda g: g.drain(fusable=False))
                    P.op("dve", lambda g, i=it: g.tensor_mul(
                        agv[:, i * TH:(i + 1) * TH],
                        agv[:, i * TH:(i + 1) * TH], combB[:]),
                        incs=[(S.dve, 1)])
                    gu_dma[key] = S.dve.n
                dv_gu_all = S.dve.n
                dv_prev_gu_all = dv_gu_all
                if hh == 0 and e == 0:
                    if "dbg_gu" in dbg_d:
                        sdma(dbg_d["dbg_gu"].ap(), agv[:, 0:NI * TH],
                             waits=[(S.dve, dv_gu_all)])
                    if "dbg_cbB" in dbg_d:
                        sdma(dbg_d["dbg_cbB"].ap(), combB[:],
                             waits=[(S.dve, dv_gu_all)])

                for ht in range(HT):
                    slot = ht % 2
                    wts = []
                    if slot in wd_slot_free:
                        wts.append((S.pe, wd_slot_free[slot]))
                    wd = wdma(wd_sl(slot), mwd.ap()[e, ht], waits=wts,
                              sem=S_wd[slot])
                    ysl = ht % 2
                    yw = [wd, (S.dve, dv_gu_all)]
                    if (hh, e, ht - 2) in y_evac:
                        yw.append((S.dve, y_evac[(hh, e, ht - 2)]))
                    elif e > 0 and (hh, e - 1, HT - 2 + (ht % 2)) in y_evac:
                        yw.append((S.dve, y_evac[(hh, e - 1, HT - 2 + (ht % 2))]))
                    for ch in range(2):
                        for it in range(NI):
                            P.op("pe", lambda g, c=ch, i=it, s=slot: g.matmul(
                                p_y(s)[:, c * 512:(c + 1) * 512],
                                wd_sl(s)[:, i * 128:(i + 1) * 128],
                                agv[:, i * TH + c * 512: i * TH + (c + 1) * 512],
                                start=(i == 0), stop=(i == NI - 1)),
                                waits=yw if (ch == 0 and it == 0) else [],
                                incs=[(S.pe, 1)] if (ch == 1 and it == NI - 1)
                                else [])
                    wd_slot_free[slot] = S.pe.n
                    pe_y = S.pe.n
                    ytile = abig.ap()[:, ht * T + tof: ht * T + tof + TH]
                    if e == 0:
                        P.op("dve", lambda g, yt=ytile, s=ysl: g.tensor_copy(
                            yt, p_y(s)), waits=[(S.pe, pe_y)],
                            incs=[(S.dve, 1)])
                    else:
                        P.op("dve", lambda g, yt=ytile, s=ysl: g.tensor_tensor(
                            yt, yt, p_y(s), ALU.add), waits=[(S.pe, pe_y)],
                            incs=[(S.dve, 1)])
                    y_evac[(hh, e, ht)] = S.dve.n
                last_y_dve = S.dve.n
                if hh == 0 and e == 0 and "dbg_y0" in dbg_d:
                    sdma(dbg_d["dbg_y0"].ap(), abig.ap()[:],
                         waits=[(S.dve, last_y_dve)])

        dv_m_done = S.dve.n
        for ht in (range(HT) if (m_halves == 2 and m_experts == E) else []):
            P.op("act", lambda g, h=ht: g.activation(
                abig.ap()[:, h * T:(h + 1) * T], abig.ap()[:, h * T:(h + 1) * T],
                AF.Silu, bias=0.0, scale=1.0),
                waits=[(S.dve, dv_m_done)] if ht == 0 else [],
                incs=[(S.act, 1)])
            a = S.act.n
            sdma(gspill.ap()[ht], abig.ap()[:, ht * T:(ht + 1) * T],
                 waits=[(S.act, a)])
        w_spill = S.sp.n
        if "dbg_y" in dbg_d:
            sdma(dbg_d["dbg_y"].ap(), abig.ap()[:], waits=[(S.act, S.act.n)])

        barrier()

        if upto < 3:
            with nc.Block() as block:
                P.emit(block)
            return nc
        for e in ("act", "dve", "pe", "gp", "sp"):
            P.op(e, lambda g: g.nop(), waits=[(S.sp, S.sp.n)])
        act_xnp = ln_phase(xTp, abig.ap(), do_router=False)
        def head_weights(head, feats, waits):
            ws = []
            for f, slot in feats:
                ws.append(wdma(wq_slot(slot), wqkv.ap()[head, f],
                               waits=waits, sem=S_wq[slot]))
                waits = []
            return ws

        def project(dst, wslot, src_arena, pe_waits, sil):
            for ch in range(4):
                for ht in range(HT):
                    P.op("pe", lambda g, c=ch, h=ht: g.matmul(
                        p_proj[:, c * 512:(c + 1) * 512],
                        wq_slot(wslot)[:, h * 128:(h + 1) * 128],
                        src_arena[:, h * T + c * 512: h * T + (c + 1) * 512],
                        start=(h == 0), stop=(h == HT - 1)),
                        waits=pe_waits if (ch == 0 and ht == 0) else [],
                        incs=[(S.pe, 1)] if (ch == 3 and ht == HT - 1) else [])
            pe_p = S.pe.n
            if sil:
                P.op("act", lambda g, d=dst: g.activation(
                    d, p_proj[:], AF.Silu, bias=0.0, scale=1.0),
                    waits=[(S.pe, pe_p)], incs=[(S.act, 1)])
            else:
                P.op("act", lambda g, d=dst: g.copy(d, p_proj[:]),
                     waits=[(S.pe, pe_p)], incs=[(S.act, 1)])
            return S.act.n, pe_p

        def head_decays(head, need_q):
            nscol = nsl[:, head:head + 1]
            if "dec" in skip:
                P.op("act", lambda g: g.nop(), incs=[(S.act, 1)], no_auto=True)
                return S.act.n
            if need_q:
                P.op("act", lambda g, ns=nscol: g.activation(
                    decq(), a1r[:], AF.Exp, bias=0.0, scale=ns))
                P.op("act", lambda g, ns=nscol: g.activation(
                    dect()[:, 0:BLK], m1c[:], AF.Exp, bias=0.0, scale=ns))
                P.op("act", lambda g, ns=nscol: g.activation(
                    dect()[:, BLK:2 * BLK], m2c[:], AF.Exp, bias=0.0, scale=ns))
            P.op("act", lambda g, ns=nscol: g.activation(
                deck[:], a2c[:], AF.Exp, bias=0.0, scale=ns))
            P.op("act", lambda g, ns=nscol: g.activation(
                bdcol[:], c256c[:], AF.Exp, bias=0.0, scale=ns),
                incs=[(S.act, 1)])
            return S.act.n

        qkd_carry = [0]

        def scan_blocks(head, src_k, src_v, with_q, seed_from, store_seed,
                        act_kv_ready, o_dst=None, nblk=NBLK):
            iw = [(S.act, S.act.n)]
            if seed_from is None:
                P.op("dve", lambda g: g.memset(kvf[:], 0.0), waits=iw)
                P.op("dve", lambda g: g.memset(kvb[:, 0:DH], 0.0),
                     incs=[(S.dve, 1)])
            else:
                P.op("dve", lambda g, s=seed_from: g.tensor_copy(kvf[:], s),
                     waits=iw)
                P.op("dve", lambda g, s=seed_from: g.tensor_copy(
                    kvb[:, 0:DH], s), incs=[(S.dve, 1)])
            dv_kv = S.dve.n
            act_cast = S.act.n
            prev_tr_evac = 0
            prev_o_evac = {}
            prev_qkd = qkd_carry[0]
            st1 = {}
            pe_o_last = [0]

            def stage1(j):
                nonlocal prev_tr_evac, prev_qkd
                sl = j % 2
                t0 = j * BLK
                r = {}
                if "tr" not in skip:
                    trw = [(S.act, act_kv_ready)]
                    if prev_tr_evac:
                        trw = [(S.dve, prev_tr_evac[0]),
                               (S.act, prev_tr_evac[1])]
                    for ss in range(2):
                        P.op("pe", lambda g, s=ss, t=t0: g.transpose(
                            p_tr[:, s * 128:(s + 1) * 128],
                            src_k[:, t + s * 128: t + (s + 1) * 128], idb[:]),
                            waits=trw if ss == 0 else [])
                        trw = []
                    for ss in range(2):
                        P.op("pe", lambda g, s=ss, t=t0: g.transpose(
                            p_trv[:, s * 128: (s + 1) * 128],
                            src_v[:, t + s * 128: t + (s + 1) * 128], idb[:]),
                            incs=[(S.pe, 1)] if ss == 1 else [])
                    pe_tr = S.pe.n
                    if "evk" in skip:
                        P.op("dve", lambda g: g.engine_nop(),
                             waits=[(S.pe, pe_tr)], incs=[(S.dve, 1)])
                    else:
                        for ss in range(2):
                            P.op("dve", lambda g, s=ss, q=sl: g.tensor_scalar(
                                ktm(q)[:, s * 128:(s + 1) * 128],
                                p_tr[:, s * 128:(s + 1) * 128],
                                deck[:, s:s + 1], None, ALU.mult),
                                waits=[(S.pe, pe_tr)] if ss == 0 else [],
                                incs=[(S.dve, 1)] if ss == 1 else [])
                    r["dv_ktm"] = S.dve.n
                    if "evv" in skip:
                        P.op("act", lambda g: g.nop(),
                             waits=[(S.pe, pe_tr)], incs=[(S.act, 1)],
                             no_auto=True)
                    else:
                        for ss in range(2):
                            P.op("act", lambda g, s=ss, q=sl: g.copy(
                                vtok(q)[:, s * 128:(s + 1) * 128],
                                p_trv[:, s * 128: (s + 1) * 128]),
                                waits=[(S.pe, pe_tr)] if ss == 0 else [],
                                incs=[(S.act, 1)] if ss == 1 else [])
                    r["act_vt"] = S.act.n
                    prev_tr_evac = (r["dv_ktm"], r["act_vt"])
                if with_q:
                    qw = [(S.pe, pe_o_last[0])] if pe_o_last[0] else []
                    P.op("dve", lambda g, q=sl, t=t0: g.tensor_mul(
                        qdb(q), qtile()[:, t:t + BLK], decq()),
                        waits=qw, incs=[(S.dve, 1)])
                    r["dv_qd"] = S.dve.n
                    qkw = []
                    if prev_qkd:
                        qkw.append((S.dve, prev_qkd))
                    for ss in range(2):
                        P.op("pe", lambda g, s=ss, t=t0: g.matmul(
                            p_qk[:, s * 256:(s + 1) * 256],
                            src_k[:, t + s * 128: t + (s + 1) * 128],
                            qtile()[:, t:t + BLK], start=True, stop=True),
                            waits=qkw if ss == 0 else [],
                            incs=[(S.pe, 1)] if ss == 1 else [])
                    pe_qk = S.pe.n
                    for ss in range(2):
                        P.op("dve", lambda g, s=ss, q=sl: g.tensor_tensor(
                            qkdb(q)[:, s * 256:(s + 1) * 256],
                            p_qk[:, s * 256:(s + 1) * 256],
                            dect()[:, s * 256:(s + 1) * 256], ALU.mult),
                            waits=[(S.pe, pe_qk)] if ss == 0 else [],
                            incs=[(S.dve, 1)] if ss == 1 else [])
                    prev_qkd = S.dve.n
                    qkd_carry[0] = S.dve.n
                    r["dv_qkd"] = S.dve.n
                return r

            st1[0] = stage1(0)
            for j in range(nblk):
                sl = j % 2
                t0 = j * BLK
                if j + 1 < nblk:
                    st1[j + 1] = stage1(j + 1)
                r = st1[j]
                if with_q and "tr" not in skip:
                    ow = [(S.dve, r["dv_qd"]), (S.act, act_cast)]
                    if j >= 2 and (j - 2) in prev_o_evac:
                        ow.append((S.act, prev_o_evac[j - 2]))
                    P.op("pe", lambda g, q=sl: g.matmul(
                        p_o(q), kvb[:, q * DH:(q + 1) * DH], qdb(q),
                        start=True, stop=False), waits=ow)
                    for ss in range(2):
                        P.op("pe", lambda g, s=ss, q=sl: g.matmul(
                            p_o(q), vtok(q)[:, s * 128:(s + 1) * 128],
                            qkdb(q)[:, s * 256:(s + 1) * 256],
                            start=False, stop=(s == 1)),
                            waits=[(S.dve, r["dv_qkd"]), (S.dve, r["dv_ktm"]),
                                   (S.act, r["act_vt"])] if ss == 0 else [],
                            incs=[(S.pe, 1)] if ss == 1 else [])
                    pe_o = S.pe.n
                    pe_o_last[0] = pe_o
                    P.op("act", lambda g, q=sl, t=t0, od=o_dst: g.copy(
                        od[:, t:t + BLK], p_o(q)),
                        waits=[(S.pe, pe_o)], incs=[(S.act, 1)])
                    prev_o_evac[j] = S.act.n
                if "tr" in skip or "amm" in skip:
                    continue
                aw = [(S.dve, r["dv_ktm"]), (S.act, r["act_vt"]),
                      (S.dve, dv_kv)]
                for ss in range(2):
                    P.op("pe", lambda g, s=ss, q=sl: g.matmul(
                        p_A2, ktm(q)[:, s * 128:(s + 1) * 128],
                        vtok(q)[:, s * 128:(s + 1) * 128],
                        start=(s == 0), stop=(s == 1)),
                        waits=aw if ss == 0 else [],
                        incs=[(S.pe, 1)] if ss == 1 else [])
                pe_A = S.pe.n
                if "kvu" in skip:
                    continue
                P.op("dve", lambda g: g.drain(fusable=False))
                P.op("dve", lambda g: g.scalar_tensor_tensor(
                    kvf[:], kvf[:], bdcol[:, 0:1], p_A2, ALU.mult, ALU.add),
                    waits=[(S.pe, pe_A)], incs=[(S.dve, 1)])
                dv_kv = S.dve.n
                P.op("act", lambda g, q=(j + 1) % 2: g.copy(
                    kvb[:, q * DH:(q + 1) * DH], kvf[:]),
                    waits=[(S.dve, dv_kv)], incs=[(S.act, 1)])
                act_cast = S.act.n
            if store_seed is not None and "seed" not in skip:
                P.op("act", lambda g, d=store_seed: g.copy(d, kvf[:]),
                     waits=[(S.dve, dv_kv)], incs=[(S.act, 1)])
            return S.act.n

        prev_proj_pe = 0
        for head in range(a0_heads):
            slk = (head % 2) * 2
            wts = [(S.act, act_xnp)] if head == 0 else []
            if head >= 2:
                wts.append((S.pe, prev_proj_pe))
            wk = head_weights(head, [(1, slk), (2, slk + 1)], wts)
            dec_a = head_decays(head, need_q=False)
            pw = [wk[0], wk[1]]
            if head > 0:
                pw.append((S.act, S.act.n))
                pw.append((S.dve, S.dve.n))
            a_k, _ = project(ktile(), slk, abig.ap(), pw, sil=True)
            a_v, pe_v = project(vtile(), slk + 1, abig.ap(), [(S.act, a_k)],
                                sil=False)
            prev_proj_pe = pe_v
            scan_blocks(head, ktile(), vtile(), with_q=False,
                        seed_from=None,
                        store_seed=kvseed[:, head * DH:(head + 1) * DH],
                        act_kv_ready=a_v, nblk=a0_blocks)
        if "dbg_seed" in dbg_d:
            sdma(dbg_d["dbg_seed"].ap(), kvseed[:], waits=[(S.act, S.act.n)])

        barrier()

        if upto < 4:
            with nc.Block() as block:
                P.emit(block)
            return nc
        prev_proj_pe = 0
        for head in range(a_heads):
            slk = (head % 3)
            wts = []
            if head >= 1:
                wts.append((S.pe, prev_proj_pe))
            wq = head_weights(head, [(0, 0), (1, 1), (2, 2)], wts)
            dec_a = head_decays(head, need_q=True)
            pw = [wq[0], wq[1], wq[2]]
            if head > 0:
                pw.append((S.act, S.act.n))
                pw.append((S.dve, S.dve.n))
            a_q, _ = project(qtile(), 0, xn.ap(), pw, sil=True)
            a_k, _ = project(ktile(), 1, xn.ap(), [(S.act, a_q)], sil=True)
            a_v, pe_v = project(vtile(), 2, xn.ap(), [(S.act, a_k)], sil=False)
            prev_proj_pe = pe_v
            o_dst = abig.ap()[:, head * T:(head + 1) * T]
            scan_blocks(head, ktile(), vtile(), with_q=True,
                        seed_from=kvseed[:, head * DH:(head + 1) * DH],
                        store_seed=None, act_kv_ready=a_v, o_dst=o_dst)
        act_o_done = S.act.n
        if "dbg_o" in dbg_d:
            sdma(dbg_d["dbg_o"].ap(), abig.ap()[:], waits=[(S.act, act_o_done)])

        barrier()

        if upto < 5:
            with nc.Block() as block:
                P.emit(block)
            return nc
        dv_sq = {}
        for ht in range(HT):
            sl = ht % 2
            sq_w = []
            if ht >= 2:
                sq_w.append((S.pe, dv_sq[ht - 2]))
            P.op("dve", lambda g, h=ht, s=sl: g.tensor_mul(
                sqg(s), abig.ap()[:, h * T:(h + 1) * T],
                abig.ap()[:, h * T:(h + 1) * T]),
                waits=sq_w, incs=[(S.dve, 1)])
            dvn = S.dve.n
            for ch in range(4):
                P.op("pe", lambda g, c=ch, h=ht, s=sl: g.matmul(
                    p_r[:, c * 512:(c + 1) * 512], onesb[:],
                    sqg(s)[:, c * 512:(c + 1) * 512],
                    start=(h == 0), stop=(h == HT - 1)),
                    waits=[(S.dve, dvn)] if ch == 0 else [],
                    incs=[(S.pe, 1)] if ch == 3 else [])
            dv_sq[ht] = S.pe.n
        pe_r = S.pe.n
        for ch in range(4):
            P.op("dve", lambda g, c=ch: g.tensor_scalar(
                astr2.ap()[:, c * 512:(c + 1) * 512],
                p_r[:, c * 512:(c + 1) * 512], 1.0 / 2048.0, None, ALU.mult),
                waits=[(S.pe, pe_r)] if ch == 0 else [],
                incs=[(S.dve, 1)] if ch == 3 else [])
        dv_rt = S.dve.n
        P.op("act", lambda g: g.activation(
            astr.ap()[:], astr2.ap()[:], AF.Ln, bias=1e-6, scale=1.0),
            waits=[(S.dve, dv_rt)])
        P.op("act", lambda g: g.drain(fusable=False))
        P.op("act", lambda g: g.activation(
            rbv(), astr.ap()[:], AF.Exp, bias=0.0, scale=-0.5),
            incs=[(S.act, 1)])
        act_rb = S.act.n

        prev_upe = 0
        prev_act_u = 0
        dve_gu = {}
        for ht in range(HT):
            sl = ht % 2
            wts = [(S.act, act_rb)] if ht == 0 else []
            if ht >= 2:
                wts.append((S.pe, prev_upe))
            wu_ = wdma(uwg(sl), wuat.ap()[ht], waits=wts, sem=S_ug[sl])
            uw = [wu_]
            if prev_act_u:
                uw.append((S.act, prev_act_u))
            else:
                uw.append((S.act, act_rb))
            if ht >= 2:
                uw.append((S.dve, dve_gu[ht - 2]))
            for ch in range(4):
                for h2 in range(HT):
                    P.op("pe", lambda g, c=ch, h=h2, s=sl: g.matmul(
                        p_ug[:, c * 512:(c + 1) * 512],
                        uwg(s)[:, h * 128:(h + 1) * 128],
                        xn.ap()[:, h * T + c * 512: h * T + (c + 1) * 512],
                        start=(h == 0), stop=(h == HT - 1)),
                        waits=uw if (ch == 0 and h2 == 0) else [],
                        incs=[(S.pe, 1)] if (ch == 3 and h2 == HT - 1) else [])
            prev_upe = S.pe.n
            pe_u = S.pe.n
            P.op("act", lambda g, s=sl: g.copy(utg(s), p_ug[:]),
                 waits=[(S.pe, pe_u)], incs=[(S.act, 1)])
            prev_act_u = S.act.n
            act_u = S.act.n
            P.op("dve", lambda g, h=ht, s=sl: g.tensor_mul(
                gutv(s), abig.ap()[:, h * T:(h + 1) * T], rbv()),
                waits=[(S.act, act_u)])
            P.op("dve", lambda g: g.drain(fusable=False))
            P.op("dve", lambda g, h=ht, s=sl: g.tensor_mul(
                abig.ap()[:, h * T:(h + 1) * T], gutv(s), utg(s)),
                incs=[(S.dve, 1)])
            dve_gu[ht] = S.dve.n
        dv_gu_att = S.dve.n

        prev_spe = 0
        prev_out_dve = 0
        prev_store = 0
        for ot in range(HT):
            sl = ot % 2
            wts = [(S.dve, dv_gu_att)] if ot == 0 else []
            if ot >= 2:
                wts.append((S.pe, prev_spe))
            wo = wdma(woutg(sl), wout.ap()[ot], waits=wts, sem=S_wo[sl])
            wg_ = sdma(astr.ap()[:], xT.ap()[ot * 128:(ot + 1) * 128, :],
                       waits=[(S.dve, prev_out_dve)] if ot >= 1 else
                       [(S.act, act_rb)], sem=S.x)
            wb_ = gdma(gbk(), gspill.ap()[ot],
                       waits=[(S.dve, prev_out_dve)] if ot >= 1 else
                       [(S.sp, w_spill)], sem=S.gb)
            sw = [wo, (S.dve, dv_gu_att)]
            if ot >= 1:
                sw.append((S.dve, prev_out_dve))
            for ch in range(4):
                for h2 in range(HT):
                    P.op("pe", lambda g, c=ch, h=h2, s=sl: g.matmul(
                        p_s[:, c * 512:(c + 1) * 512],
                        woutg(s)[:, h * 128:(h + 1) * 128],
                        abig.ap()[:, h * T + c * 512: h * T + (c + 1) * 512],
                        start=(h == 0), stop=(h == HT - 1)),
                        waits=sw if (ch == 0 and h2 == 0) else [],
                        incs=[(S.pe, 1)] if (ch == 3 and h2 == HT - 1) else [])
            prev_spe = S.pe.n
            pe_s = S.pe.n
            dw = [(S.pe, pe_s), wb_, wg_]
            if ot == 0:
                dw.append((S.act, act_rb))
            else:
                dw.append((S.sp, prev_store))
            P.op("dve", lambda g: g.tensor_tensor(
                astr2.ap()[:], p_s[:], gbk(), ALU.mult), waits=dw)
            P.op("dve", lambda g: g.drain(fusable=False))
            P.op("dve", lambda g: g.tensor_add(
                astr2.ap()[:], astr2.ap()[:], astr.ap()[:]),
                incs=[(S.dve, 1)])
            prev_out_dve = S.dve.n
            prev_store = sdma(out_d.ap()[ot * 128:(ot + 1) * 128, :],
                              astr2.ap()[:], waits=[(S.dve, prev_out_dve)])
        w_out_done = S.sp.n
        P.op("sp", lambda g: g.nop(), waits=[(S.sp, w_out_done)])

        with nc.Block() as block:
            P.emit(block)

    return nc



def make_in_maps(L0_attn_x, L0_attn_slope_rate, L0_attn_W_qkvu, L0_attn_W_out,
                 L0_ffn_Wr, L0_ffn_Wg, L0_ffn_Wu, L0_ffn_Wd):
    x = np.asarray(L0_attn_x, dtype=np.float32)
    shared = stage_shared(L0_attn_W_qkvu, L0_attn_W_out, L0_ffn_Wr,
                          L0_ffn_Wg, L0_ffn_Wu, L0_ffn_Wd,
                          L0_attn_slope_rate)
    consts = stage_consts()
    zeros = np.zeros((D, T), np.float32)
    in_maps = []
    for c in range(8):
        b, half = c // 2, c % 2
        m = dict(shared)
        m.update(consts)
        m["xT"] = np.ascontiguousarray(x[b, half * T:(half + 1) * T, :].T)
        m["xTp"] = (np.ascontiguousarray(x[b, 0:T, :].T) if half == 1
                    else zeros)
        m["below"] = (np.arange(8) < c).astype(np.float32).reshape(8, 1)
        in_maps.append(m)
    return in_maps


def kernel(L0_attn_x, L0_attn_slope_rate, L0_attn_W_qkvu, L0_attn_W_out,
           L0_ffn_Wr, L0_ffn_Wg, L0_ffn_Wu, L0_ffn_Wd):
    in_maps = make_in_maps(L0_attn_x, L0_attn_slope_rate, L0_attn_W_qkvu,
                           L0_attn_W_out, L0_ffn_Wr, L0_ffn_Wg, L0_ffn_Wu,
                           L0_ffn_Wd)
    nc = build_nc()
    res = run_bass_kernel_spmd(nc, in_maps, core_ids=list(range(8)))
    out = np.empty((4, 4096, D), np.float32)
    for c in range(8):
        b, half = c // 2, c % 2
        out[b, half * T:(half + 1) * T, :] = res.results[c]["out"].T
    return out


# version 06 (final)
# speedup vs baseline: 1.0409x; 1.0395x over previous
import numpy as np
import ml_dtypes
import concourse.bass as bass
import concourse.mybir as mybir
from concourse.bass_utils import run_bass_kernel_spmd
from contextlib import ExitStack

F32 = mybir.dt.float32
BF16 = mybir.dt.bfloat16
AF = mybir.ActivationFunctionType
ALU = mybir.AluOpType
AX = mybir.AxisListType.X

D = 2048
T = 2048
NT = 16
HT = 16
NH = 16
DH = 128
BLK = 256
NBLK = 8
E = 8
NI = 11
TH = 1024
CAP = 5120
BIG = 1.0e9

ENGINES = ("gp", "pe", "act", "dve", "sp")


class Sem:
    def __init__(self, nc, ctx, name):
        self.h = ctx.enter_context(nc.semaphore(name))
        self.n = 0

    def bump(self, k=1):
        self.n += k
        return self.n


class Prog:

    def __init__(self):
        self.streams = {e: [] for e in ENGINES}
        self.auto = {}

    def op(self, eng, fn, waits=(), incs=(), no_auto=False):
        waits = list(waits)
        incs = list(incs)
        a = None if no_auto else self.auto.get(eng)
        if a is not None:
            if a.n > 0:
                waits.append((a, a.n))
            if not any(sem is a for sem, _ in incs):
                incs.append((a, 1))
        for sem, k in incs:
            sem.bump(k)
        self.streams[eng].append((waits, fn, incs))

    def emit(self, block):
        def make(engine_name):
            def body(eng):
                for waits, fn, incs in self.streams[engine_name]:
                    for sem, val in waits:
                        eng.wait_ge(sem.h, val)
                    instr = fn(eng)
                    for sem, k in incs:
                        if instr is None:
                            instr = eng.nop()
                        instr = instr.then_inc(sem.h, k)
            return body
        block.gpsimd(make("gp"))
        block.tensor(make("pe"))
        block.scalar(make("act"))
        block.vector(make("dve"))
        block.sync(make("sp"))


def _bf(a):
    return np.ascontiguousarray(a).astype(ml_dtypes.bfloat16)


def stage_shared(W_qkvu, W_out, Wr, Wg, Wu, Wd, slope):
    st = {}
    W4 = np.asarray(W_qkvu, dtype=np.float32).reshape(4, NH, DH, HT, 128)
    st["wqkv"] = np.ascontiguousarray(W4[0:3].transpose(1, 0, 3, 4, 2))
    st["wuat"] = np.ascontiguousarray(W4[3].transpose(0, 2, 3, 1))
    st["wr"] = np.ascontiguousarray(
        np.asarray(Wr, dtype=np.float32).T.reshape(HT, 128, E))
    A = np.asarray(Wg, dtype=np.float32).transpose(0, 2, 1)
    wg5 = A.reshape(E, HT, 128, NI, 128).transpose(0, 3, 1, 2, 4)
    A = np.asarray(Wu, dtype=np.float32).transpose(0, 2, 1)
    wu5 = A.reshape(E, HT, 128, NI, 128).transpose(0, 3, 1, 2, 4)
    st["mwgu"] = np.ascontiguousarray(np.stack([wg5, wu5], axis=2))
    A = np.asarray(Wd, dtype=np.float32).transpose(0, 2, 1)
    st["mwd"] = np.ascontiguousarray(
        A.reshape(E, NI, 128, HT, 128).transpose(0, 3, 1, 2, 4))
    A = np.asarray(W_out, dtype=np.float32).T
    st["wout"] = np.ascontiguousarray(
        A.reshape(HT, 128, HT, 128).transpose(2, 0, 1, 3))
    s = np.asarray(slope, dtype=np.float32).reshape(1, NH)
    st["negslope"] = np.ascontiguousarray(np.tile(-s, (128, 1)))
    return st


def stage_consts():
    st = {}
    st["tri128"] = np.ascontiguousarray(
        (np.arange(128)[:, None] <= np.arange(128)[None, :]).astype(np.float32))
    st["tri16x"] = np.ascontiguousarray(
        (np.arange(16)[:, None] < np.arange(16)[None, :]).astype(np.float32))
    st["ones128b"] = _bf(np.ones((128, 128), np.float32))
    st["id128b"] = _bf(np.eye(128, dtype=np.float32))
    st["onescol"] = np.ones((128, 1), np.float32)
    st["ones16col"] = np.ones((16, 1), np.float32)
    st["ones16row"] = np.ones((1, 16), np.float32)
    st["ones8col"] = np.ones((8, 1), np.float32)
    st["ones1row"] = np.ones((1, 128), np.float32)
    st["ones1rowb"] = _bf(np.ones((1, 128), np.float32))
    st["arange8"] = np.ascontiguousarray(
        np.tile(np.arange(8, dtype=np.float32)[None, :], (128, 1)))
    i1 = np.arange(1, BLK + 1, dtype=np.float32)
    st["a1row"] = _bf(np.tile(i1[None, :], (128, 1)))
    a2 = np.stack([BLK - i1[0:128], BLK - i1[128:256]], axis=1)
    st["a2cols"] = np.ascontiguousarray(a2.astype(np.float32))
    st["c256"] = np.full((128, 1), float(BLK), np.float32)
    s_idx = np.arange(BLK)[:, None]
    t_idx = np.arange(BLK)[None, :]
    M = np.where(t_idx >= s_idx, (t_idx - s_idx).astype(np.float32), BIG)
    st["m1d"] = _bf(M[0:128])
    st["m2d"] = _bf(M[128:256])
    st["cvals"] = np.ascontiguousarray(np.tile(np.array(
        [0.0, 1.0, -0.5, 1e-5, 1e-6, 1.0 / 2048.0], np.float32)[None, :],
        (128, 1)))
    st["zeros16"] = np.zeros((16, 128), np.float32)
    return st


CONST_VALS = [0.0, 1.0, -0.5, 1e-5, 1e-6, 1.0 / 2048.0]


def build_nc(debug=(), upto=5, m_experts=E, m_halves=2, a0_heads=NH, a_heads=NH, a0_blocks=NBLK, skip=()):
    nc = bass.Bass("TRN2", target_bir_lowering=False)
    dbg = set(debug)

    dram = {}
    def din(name, shape, dt):
        t = nc.dram_tensor(name, shape, dt, kind="ExternalInput")
        dram[name] = t
        return t

    xT = din("xT", [D, T], F32)
    xTp = din("xTp", [D, T], F32)
    wqkv = din("wqkv", [NH, 3, HT, 128, 128], F32)
    wuat = din("wuat", [HT, HT, 128, 128], F32)
    wr = din("wr", [HT, 128, E], F32)
    mwgu = din("mwgu", [E, NI, 2, HT, 128, 128], F32)
    mwd = din("mwd", [E, HT, NI, 128, 128], F32)
    wout = din("wout", [HT, HT, 128, 128], F32)
    negslope = din("negslope", [128, NH], F32)
    below = din("below", [8, 1], F32)
    tri128 = din("tri128", [128, 128], F32)
    tri16x = din("tri16x", [16, 16], F32)
    ones128b = din("ones128b", [128, 128], BF16)
    id128b = din("id128b", [128, 128], BF16)
    onescol = din("onescol", [128, 1], F32)
    ones16col = din("ones16col", [16, 1], F32)
    ones16row = din("ones16row", [1, 16], F32)
    ones8col = din("ones8col", [8, 1], F32)
    ones1row = din("ones1row", [1, 128], F32)
    ones1rowb = din("ones1rowb", [1, 128], BF16)
    arange8 = din("arange8", [128, 8], F32)
    a1row = din("a1row", [128, BLK], BF16)
    a2cols = din("a2cols", [128, 2], F32)
    c256 = din("c256", [128, 1], F32)
    m1d = din("m1d", [128, BLK], BF16)
    m2d = din("m2d", [128, BLK], BF16)
    cvals = din("cvals", [128, 6], F32)
    zeros16 = din("zeros16", [16, 128], F32)

    out_d = nc.dram_tensor("out", [D, T], F32, kind="ExternalOutput")
    gspill = nc.dram_tensor("gspill", [HT, 128, T], BF16, kind="Internal")
    rsc = nc.dram_tensor("rsc", [16, 16], F32, kind="Internal")
    ccin = nc.dram_tensor("ccin", [16, 128], F32, kind="Internal")
    ccout = nc.dram_tensor("ccout", [128, 128], F32, kind="Internal")

    dbg_d = {}
    for name, shape, dt in [
        ("dbg_xn", [128, HT * T], BF16), ("dbg_comb", [128, NT * 8], F32),
        ("dbg_y", [128, HT * T], BF16), ("dbg_o", [128, NH * T], BF16),
        ("dbg_logits", [128, NT * 8], F32), ("dbg_mr", [128, 2 * T], BF16),
        ("dbg_seed", [128, NH * 128], BF16), ("dbg_gath", [8, 16], F32),
        ("dbg_crow1", [1, 256], F32), ("dbg_gu", [128, NI * TH], BF16),
        ("dbg_w", [128, 2048], BF16), ("dbg_gt", [128, 2 * TH], BF16),
        ("dbg_cbB", [128, TH], BF16), ("dbg_y0", [128, HT * T], BF16),
    ]:
        if name in dbg:
            dbg_d[name] = nc.dram_tensor(name, shape, dt, kind="ExternalOutput")

    ctx = ExitStack()
    with ctx:
        sb = lambda n, s, d: ctx.enter_context(nc.sbuf_tensor(n, s, d))
        psal = lambda n, s, d: ctx.enter_context(nc.psum_tensor(n, s, d))

        xn = sb("xn", [128, HT * T], BF16)
        abig = sb("abig", [128, HT * T], BF16)
        agu = sb("agu", [128, NI * TH], BF16)
        wstr = sb("wstr", [128, 8192], BF16)
        astr = sb("astr", [128, T], F32)
        astr2 = sb("astr2", [128, T], F32)
        mr = sb("mr", [128, 2 * T], BF16)
        combT = sb("combT", [8, T], BF16)
        crow = sb("crow", [1, TH], BF16)
        combB = sb("combB", [128, TH], BF16)
        kvseed = sb("kvseed", [128, NH * DH], BF16)
        kvf = sb("kvf", [128, DH], F32)
        kvb = sb("kvb", [128, 2 * DH], BF16)
        deck = sb("deck", [128, 2], F32)
        bdcol = sb("bdcol", [128, 1], F32)
        tri = sb("tri", [128, 128], F32)
        trix = sb("trix", [16, 16], F32)
        onesb = sb("onesb", [128, 128], BF16)
        idb = sb("idb", [128, 128], BF16)
        onec = sb("onec", [128, 1], F32)
        one16 = sb("one16", [16, 1], F32)
        one16r = sb("one16r", [1, 16], F32)
        one8 = sb("one8", [8, 1], F32)
        one1r = sb("one1r", [1, 128], F32)
        one1rb = sb("one1rb", [1, 128], BF16)
        ar8 = sb("ar8", [128, 8], F32)
        a1r = sb("a1r", [128, BLK], BF16)
        a2c = sb("a2c", [128, 2], F32)
        c256c = sb("c256c", [128, 1], F32)
        m1c = sb("m1c", [128, BLK], BF16)
        m2c = sb("m2c", [128, BLK], BF16)
        cv = sb("cv", [128, 6], F32)
        wrc = sb("wrc", [128, HT * E], F32)
        nsl = sb("nsl", [128, NH], F32)
        belowc = sb("belowc", [8, 1], F32)

        for i, v in enumerate(CONST_VALS):
            nc.const_aps.register(F32, v, cv[:, i:i + 1])

        abf = abig.ap().bitcast(F32)
        asb = astr.ap().bitcast(BF16)
        asb2 = astr2.ap().bitcast(BF16)
        logit = lambda: abf[:, 0:128]
        gatet = lambda: abf[:, 128:256]
        rtmpa = lambda: abf[:, 256:384]
        rtmpb = lambda: abf[:, 384:512]
        rcol = lambda: abf[:, 512:576]
        ind12 = lambda: abf[:, 576:832]
        compsb = lambda: abf[:, 832:960]
        cntrow = lambda: abf[0:1, 960:1216]
        cnts16 = lambda: abf[0:16, 1216:1232]
        gath = lambda: abf[0:8, 1232:1248]
        offtot = lambda: abf[0:1, 1248:1280]
        baserow = lambda: abf[0:1, 1280:1296]
        ccar = lambda: abf[0:16, 1296:1312]
        crow1 = lambda: abf[0:1, 1312:1568]
        sstk = lambda: abf[:, 1600:1728]
        compsbB = lambda: abf.bitcast(BF16)[:, 3456:3584]
        agv = agu.ap()
        qtile = lambda: agv[:, 0:2048]
        ktile = lambda: agv[:, 2048:4096]
        vtile = lambda: agv[:, 4096:6144]
        ktm = lambda s: agv[:, 6144 + s * 256: 6144 + (s + 1) * 256]
        vtok = lambda s: agv[:, 6656 + s * 256: 6656 + (s + 1) * 256]
        qdb = lambda s: agv[:, 7168 + s * 256: 7168 + (s + 1) * 256]
        qkdb = lambda s: agv[:, 7680 + s * 512: 7680 + (s + 1) * 512]
        decq = lambda: agv[:, 8704:8960]
        dect = lambda: agv[:, 8960:9472]
        sqg = lambda s: agv[:, s * 2048:(s + 1) * 2048]
        utg = lambda s: agv[:, 4096 + s * 2048: 4096 + (s + 1) * 2048]
        gutg = lambda s: agv[:, 8704 + (s % 1) * 0: 0]
        wsv = wstr.ap()
        wq_slot = lambda s: wsv[:, s * 2048:(s + 1) * 2048]
        woutg = lambda s: wsv[:, s * 2048:(s + 1) * 2048]
        uwg = lambda s: wsv[:, (2 + s) * 2048:(3 + s) * 2048]
        def wgu_sl(i):
            return (asb[:, i * 2048:(i + 1) * 2048] if i < 2
                    else asb2[:, (i - 2) * 2048:(i - 1) * 2048])
        wd_sl = lambda s: wsv[:, s * 1408:(s + 1) * 1408]
        gutv = lambda s: agv[:, s * 2048:(s + 1) * 2048]
        rbv = lambda: mr.ap()[:, 0:T]
        gbk = lambda: mr.ap()[:, T:2 * T]

        pA = psal("pA", [128, 2048], F32)
        pB = psal("pB", [128, 2048], F32)
        pAb = pA.ap().bitcast(BF16)
        pBb = pB.ap().bitcast(BF16)
        p_m = pA.ap()
        p_s2 = pB.ap()
        p_log = lambda s: pA.ap()[:, s * 512: s * 512 + 128]
        p_rank = pA.ap()[:, 128:144]
        p_sm = pA.ap()[0:1, 512:1024]
        p_c16 = pA.ap()[0:16, 1024:1040]
        p_ct = pB.ap()[0:8, :]
        p_cbB = pA.ap()[:, 0:1024]
        p_g = pA.ap()[:, 0:1024]
        p_u = pA.ap()[:, 1024:2048]
        p_y = lambda s: pB.ap()[:, s * 1024:(s + 1) * 1024]
        p_proj = pA.ap()
        p_tr = pAb[:, 0:512]
        p_trv = pAb[:, 1024:1536]
        p_A2 = pA.ap()[:, 1536:1664]
        p_o = lambda s: pB.ap()[:, s * 512:(s + 1) * 512][:, 0:256]
        p_qk = pB.ap()[:, 1024:1536]
        p_r = pA.ap()
        p_ug = pB.ap()
        p_s = pA.ap()

        sems = {}
        for nm in ["gp", "sp", "pe", "act", "dve", "cc", "bar", "st",
                   "xb0", "xb1", "x", "wg0", "wg1", "wu0", "wu1",
                   "wd0", "wd1", "cr", "wq0", "wq1", "wq2", "wq3",
                   "ug0", "ug1", "wo0", "wo1", "gb"]:
            sems[nm] = Sem(nc, ctx, "s_" + nm)
        S = type("S", (), sems)
        S_xb = [S.xb0, S.xb1]
        S_wg = [S.wg0, S.wg1]
        S_wu = [S.wu0, S.wu1]
        S_wd = [S.wd0, S.wd1]
        S_wq = [S.wq0, S.wq1, S.wq2, S.wq3]
        S_ug = [S.ug0, S.ug1]
        S_wo = [S.wo0, S.wo1]
        P = Prog()
        P.auto["dve"] = S.dve
        P.auto["act"] = S.act

        def barrier():
            for e in ENGINES:
                P.op(e, lambda g: g.drain(fusable=False))
                P.op(e, lambda g: g.nop(), incs=[(S.bar, 1)], no_auto=True)
            tgt = S.bar.n
            for e in ENGINES:
                P.op(e, lambda g: g.nop(), waits=[(S.bar, tgt)], no_auto=True)

        def wdma(dst, src, waits=(), sem=None):
            sem = sem or S.gp
            d = dst.rearrange("p (h c) -> p h c", c=128)
            sr = src.rearrange("h p c -> p h c")
            P.op("gp", lambda g, d=d, sr=sr: g.dma_start(d, sr),
                 waits=waits, incs=[(sem, 16)])
            return (sem, sem.n)

        def gdma(dst, src, waits=(), sem=None):
            sem = sem or S.gp
            P.op("gp", lambda g, d=dst, s=src: g.dma_start(d, s),
                 waits=waits, incs=[(sem, 16)])
            return (sem, sem.n) if sem is not S.gp else sem.n

        def sdma(dst, src, waits=(), sem=None):
            sem = sem or S.sp
            P.op("sp", lambda g, d=dst, s=src: g.dma_start(d, s),
                 waits=waits, incs=[(sem, 16)])
            return (sem, sem.n) if sem is not S.sp else sem.n

        wl = 0
        for dst, src in [
            (tri[:], tri128.ap()), (trix[:], tri16x.ap()),
            (onesb[:], ones128b.ap()), (idb[:], id128b.ap()),
            (onec[:], onescol.ap()), (one16[:], ones16col.ap()),
            (one16r[:], ones16row.ap()),
            (one8[:], ones8col.ap()), (one1r[:], ones1row.ap()),
            (one1rb[:], ones1rowb.ap()),
            (ar8[:], arange8.ap()), (a1r[:], a1row.ap()), (a2c[:], a2cols.ap()),
            (c256c[:], c256.ap()), (m1c[:], m1d.ap()), (m2c[:], m2d.ap()),
            (cv[:], cvals.ap()),
            (wrc[:].rearrange("p (h e) -> p h e", e=E),
             wr.ap().rearrange("h p e -> p h e")),
            (nsl[:], negslope.ap()), (belowc[:], below.ap()),
        ]:
            wl = sdma(dst, src)
        w_const = wl
        wz_ccin = gdma(ccin.ap(), zeros16.ap())
        for e in ENGINES:
            P.op(e, lambda g: g.nop(), waits=[(S.sp, w_const)])

        def ln_phase(x_dram, xn_arena, do_router):
            pe_done = []
            for ht in range(HT):
                sl = ht % 2
                xb = asb[:, sl * 2048:(sl + 1) * 2048]
                x2 = asb2[:, sl * 2048:(sl + 1) * 2048]
                waits = [(S.pe, pe_done[ht - 2])] if ht >= 2 else []
                w = gdma(xb, x_dram.ap()[ht * 128:(ht + 1) * 128, :],
                         waits=waits, sem=S_xb[sl])
                P.op("dve", lambda g, xb=xb, x2=x2: g.tensor_mul(x2, xb, xb),
                     waits=[w], incs=[(S.dve, 1)])
                dvn = S.dve.n
                for ch in range(4):
                    P.op("pe", lambda g, xb=xb, c=ch, h=ht: g.matmul(
                        p_m[:, c * 512:(c + 1) * 512], onesb[:],
                        xb[:, c * 512:(c + 1) * 512],
                        start=(h == 0), stop=(h == HT - 1)),
                        waits=[w] if ch == 0 else [])
                for ch in range(4):
                    P.op("pe", lambda g, x2=x2, c=ch, h=ht: g.matmul(
                        p_s2[:, c * 512:(c + 1) * 512], onesb[:],
                        x2[:, c * 512:(c + 1) * 512],
                        start=(h == 0), stop=(h == HT - 1)),
                        waits=[(S.dve, dvn)] if ch == 0 else [],
                        incs=[(S.pe, 1)] if ch == 3 else [])
                pe_done.append(S.pe.n)
            pe_stats = S.pe.n
            for ch in range(4):
                P.op("dve", lambda g, c=ch: g.tensor_scalar(
                    astr2.ap()[:, c * 512:(c + 1) * 512],
                    p_s2[:, c * 512:(c + 1) * 512],
                    1.0 / 2048.0, None, ALU.mult),
                    waits=[(S.pe, pe_stats)] if ch == 0 else [])
            for ch in range(4):
                P.op("dve", lambda g, c=ch: g.tensor_scalar(
                    astr.ap()[:, c * 512:(c + 1) * 512],
                    p_m[:, c * 512:(c + 1) * 512],
                    1.0 / 2048.0, None, ALU.mult))
            P.op("dve", lambda g: g.drain(fusable=False))
            for ch in range(4):
                P.op("dve", lambda g, c=ch: g.tensor_copy(
                    mr.ap()[:, c * 512:(c + 1) * 512],
                    astr.ap()[:, c * 512:(c + 1) * 512]))
            for ch in range(4):
                P.op("dve", lambda g, c=ch: g.tensor_mul(
                    abf[:, 2048 + c * 512: 2048 + (c + 1) * 512],
                    astr.ap()[:, c * 512:(c + 1) * 512],
                    astr.ap()[:, c * 512:(c + 1) * 512]))
            for ch in range(4):
                P.op("dve", lambda g, c=ch: g.tensor_sub(
                    astr2.ap()[:, c * 512:(c + 1) * 512],
                    astr2.ap()[:, c * 512:(c + 1) * 512],
                    abf[:, 2048 + c * 512: 2048 + (c + 1) * 512]),
                    incs=[(S.dve, 1)] if ch == 3 else [])
            dv_var = S.dve.n
            P.op("act", lambda g: g.activation(
                abf[:, 2048:4096].bitcast(F32) if False else abf[:, 2048:4096],
                astr2.ap()[:], AF.Ln, bias=1e-5, scale=1.0),
                waits=[(S.dve, dv_var)])
            P.op("act", lambda g: g.drain(fusable=False))
            P.op("act", lambda g: g.activation(
                mr.ap()[:, T:2 * T], abf[:, 2048:4096], AF.Exp,
                bias=0.0, scale=-0.5), incs=[(S.act, 1)])
            act_r = S.act.n
            pe_ht = []
            dv_lacc = {}
            dv_xnf_l = {}
            aguF = agu.ap().bitcast(F32)
            xslot = [astr.ap()[:], aguF[:, 0:2048]]
            tslot = [astr2.ap()[:], aguF[:, 2048:4096]]
            xsem = [S.x, S.gb]
            for ht in range(HT):
                sl = ht % 2
                xa, ta = xslot[sl], tslot[sl]
                waits = [(S.act, act_r)]
                if ht >= 2:
                    waits = [(S.act, act_r + ht - 1),
                             (S.dve, dv_xnf_l[ht - 2])]
                    if do_router:
                        waits.append((S.pe, pe_ht[ht - 2]))
                w = sdma(xa, x_dram.ap()[ht * 128:(ht + 1) * 128, :],
                         waits=waits, sem=xsem[sl])
                P.op("dve", lambda g, xa=xa, ta=ta: g.tensor_sub(
                    ta, xa, mr.ap()[:, 0:T]),
                    waits=[w, (S.act, act_r)])
                P.op("dve", lambda g: g.drain(fusable=False))
                P.op("dve", lambda g, xa=xa, ta=ta: g.tensor_mul(
                    xa, ta, mr.ap()[:, T:2 * T]),
                    incs=[(S.dve, 1)])
                dv_xnf = S.dve.n
                dv_xnf_l[ht] = dv_xnf
                P.op("act", lambda g, h=ht, xa=xa: g.copy(
                    xn_arena[:, h * T:(h + 1) * T], xa),
                    waits=[(S.dve, dv_xnf)], incs=[(S.act, 1)])
                if do_router:
                    lsl = ht % 2
                    pw = [(S.dve, dv_xnf)]
                    if ht >= 2:
                        pw.append((S.dve, dv_lacc[ht - 2]))
                    for tt in range(NT):
                        P.op("pe", lambda g, h=ht, t=tt, q=lsl, xa=xa: g.matmul(
                            p_log(q)[:, t * E:(t + 1) * E],
                            xa[:, t * 128:(t + 1) * 128],
                            wrc[:, h * E:(h + 1) * E],
                            start=True, stop=True),
                            waits=pw if tt == 0 else [],
                            incs=[(S.pe, 1)] if tt == NT - 1 else [])
                    pe_l = S.pe.n
                    if ht == 0:
                        P.op("dve", lambda g, q=lsl: g.tensor_copy(
                            logit(), p_log(q)),
                            waits=[(S.pe, pe_l)], incs=[(S.dve, 1)])
                    else:
                        P.op("dve", lambda g, q=lsl: g.tensor_tensor(
                            logit(), logit(), p_log(q), ALU.add),
                            waits=[(S.pe, pe_l)], incs=[(S.dve, 1)])
                    dv_lacc[ht] = S.dve.n
                pe_ht.append(S.pe.n)
            return S.act.n

        act_xn = ln_phase(xT, xn.ap(), do_router=True)
        pe_logits = S.pe.n
        if "dbg_mr" in dbg_d:
            sdma(dbg_d["dbg_mr"].ap(), mr.ap()[:], waits=[(S.act, act_xn)])

        P.op("act", lambda g: g.nop(), waits=[(S.dve, S.dve.n)],
             incs=[(S.act, 1)])
        act_log = S.act.n
        LT = lambda tt: logit()[:, tt * E:(tt + 1) * E]
        GT = lambda tt: gatet()[:, tt * E:(tt + 1) * E]
        TA = lambda tt: rtmpa()[:, tt * E:(tt + 1) * E]
        TB = lambda tt: rtmpb()[:, tt * E:(tt + 1) * E]
        I1 = lambda tt: ind12()[:, tt * 16: tt * 16 + 8]
        I2 = lambda tt: ind12()[:, tt * 16 + 8: tt * 16 + 16]
        CM1 = lambda tt: rcol()[:, tt:tt + 1]
        CM2 = lambda tt: rcol()[:, NT + tt:NT + tt + 1]
        CI1 = lambda tt: rcol()[:, 2 * NT + tt:2 * NT + tt + 1]
        CI2 = lambda tt: rcol()[:, 3 * NT + tt:3 * NT + tt + 1]

        def dve_all(fn, waits_first=(), inc_last=True):
            for tt in range(NT):
                P.op("dve", lambda g, t=tt: fn(g, t),
                     waits=list(waits_first) if tt == 0 else [],
                     incs=[(S.dve, 1)] if (inc_last and tt == NT - 1) else [])
            return S.dve.n

        dve_all(lambda g, t: g.tensor_reduce(CM1(t), LT(t), AX, ALU.max,
                                             negate=True),
                waits_first=[(S.act, act_log)])
        dv1 = S.dve.n
        for tt in range(NT):
            P.op("act", lambda g, t=tt: g.activation(
                GT(t), LT(t), AF.Exp, bias=CM1(t), scale=1.0),
                waits=[(S.dve, dv1)] if tt == 0 else [],
                incs=[(S.act, 1)] if tt == NT - 1 else [])
        act2 = S.act.n
        dve_all(lambda g, t: g.tensor_reduce(CM2(t), GT(t), AX, ALU.add),
                waits_first=[(S.act, act2)], inc_last=False)
        dve_all(lambda g, t: g.reciprocal(CM2(t), CM2(t)), inc_last=False)
        dve_all(lambda g, t: g.tensor_scalar(GT(t), GT(t), CM2(t), None,
                                             ALU.mult), inc_last=False)
        dve_all(lambda g, t: g.tensor_reduce(CM1(t), GT(t), AX, ALU.max),
                inc_last=False)
        dve_all(lambda g, t: g.tensor_scalar(TA(t), GT(t), CM1(t), None,
                                             ALU.is_equal), inc_last=False)
        dve_all(lambda g, t: g.tensor_mul(TB(t), TA(t), ar8[:]), inc_last=False)
        dve_all(lambda g, t: g.tensor_scalar(TA(t), TA(t), -8.0, 8.0,
                                             ALU.mult, ALU.add), inc_last=False)
        dve_all(lambda g, t: g.tensor_add(TB(t), TB(t), TA(t)), inc_last=False)
        dve_all(lambda g, t: g.tensor_reduce(CI1(t), TB(t), AX, ALU.min),
                inc_last=False)
        dve_all(lambda g, t: g.tensor_scalar(I1(t), ar8[:], CI1(t), None,
                                             ALU.is_equal), inc_last=False)
        dve_all(lambda g, t: g.scalar_tensor_tensor(TA(t), I1(t), -1.0e30,
                                                    GT(t), ALU.mult, ALU.add),
                inc_last=False)
        dve_all(lambda g, t: g.tensor_reduce(CM2(t), TA(t), AX, ALU.max),
                inc_last=False)
        dve_all(lambda g, t: g.tensor_scalar(TA(t), TA(t), CM2(t), None,
                                             ALU.is_equal), inc_last=False)
        dve_all(lambda g, t: g.tensor_mul(TB(t), TA(t), ar8[:]), inc_last=False)
        dve_all(lambda g, t: g.tensor_scalar(TA(t), TA(t), -8.0, 8.0,
                                             ALU.mult, ALU.add), inc_last=False)
        dve_all(lambda g, t: g.tensor_add(TB(t), TB(t), TA(t)), inc_last=False)
        dve_all(lambda g, t: g.tensor_reduce(CI2(t), TB(t), AX, ALU.min),
                inc_last=False)
        dve_all(lambda g, t: g.tensor_scalar(I2(t), ar8[:], CI2(t), None,
                                             ALU.is_equal))
        dv_ind = S.dve.n

        for tt in range(NT):
            P.op("pe", lambda g, t=tt: g.matmul(
                p_sm[0:1, t * 16: t * 16 + 8], onec[:], I1(t),
                start=True, stop=True),
                waits=[(S.dve, dv_ind)] if tt == 0 else [])
            P.op("pe", lambda g, t=tt: g.matmul(
                p_sm[0:1, t * 16 + 8: t * 16 + 16], onec[:], I2(t),
                start=True, stop=True),
                incs=[(S.pe, 1)] if tt == NT - 1 else [])
        pe_cnt = S.pe.n
        P.op("act", lambda g: g.copy(cntrow(), p_sm[0:1, 0:256]),
             waits=[(S.pe, pe_cnt)], incs=[(S.act, 1)])
        act_cnt = S.act.n
        w1 = gdma(rsc.ap().rearrange("a b -> (a b)").rearrange("(o x) -> o x", o=1), cntrow(),
                  waits=[(S.act, act_cnt)])
        w2 = gdma(cnts16(), rsc.ap(), waits=[(S.gp, w1)])
        P.op("pe", lambda g: g.matmul(p_sm[0:1, 256:272], one16[:], cnts16(),
                                      start=True, stop=True),
             waits=[(S.gp, w2)], incs=[(S.pe, 1)])
        pe_tot = S.pe.n
        P.op("act", lambda g: g.copy(cntrow()[0:1, 0:16], p_sm[0:1, 256:272]),
             waits=[(S.pe, pe_tot)], incs=[(S.act, 1)])
        act_tot = S.act.n
        w3 = gdma(ccin.ap()[0:1, 0:16], cntrow()[0:1, 0:16],
                  waits=[(S.act, act_tot), (S.gp, w2)])
        P.op("gp", lambda g: g.collective_compute(
            "AllGather", ALU.bypass,
            replica_groups=[[0, 1, 2, 3, 4, 5, 6, 7]],
            ins=[ccin.ap().opt()], outs=[ccout.ap().opt()]),
            waits=[(S.gp, w3)], incs=[(S.cc, 1)])
        w4 = gdma(gath(), ccout.ap().rearrange("(c r) f -> c r f", r=16)[:, 0, 0:16],
                  waits=[(S.cc, 1)])
        if "dbg_gath" in dbg_d:
            sdma(dbg_d["dbg_gath"].ap(), gath(), waits=[(S.gp, w4)])
        P.op("pe", lambda g: g.matmul(p_sm[0:1, 272:288], belowc[:], gath(),
                                      start=True, stop=True),
             waits=[(S.gp, w4)])
        P.op("pe", lambda g: g.matmul(p_sm[0:1, 288:304], one8[:], gath(),
                                      start=True, stop=True),
             incs=[(S.pe, 1)])
        pe_off = S.pe.n
        P.op("act", lambda g: g.copy(offtot(), p_sm[0:1, 272:304]),
             waits=[(S.pe, pe_off)], incs=[(S.act, 1)])
        act_off = S.act.n
        P.op("dve", lambda g: g.tensor_copy(baserow()[0:1, 0:8],
                                            offtot()[0:1, 0:8]),
             waits=[(S.act, act_off)])
        P.op("dve", lambda g: g.tensor_add(
            baserow()[0:1, 8:16], offtot()[0:1, 8:16], offtot()[0:1, 16:24]),
            incs=[(S.dve, 1)])
        dv_base = S.dve.n
        P.op("pe", lambda g: g.matmul(p_c16, trix[:], cnts16(),
                                      start=True, stop=False),
             waits=[(S.dve, dv_base)])
        P.op("pe", lambda g: g.matmul(p_c16, one16r[:], baserow(),
                                      start=False, stop=True),
             incs=[(S.pe, 1)])
        pe_car = S.pe.n
        P.op("act", lambda g: g.copy(ccar(), p_c16),
             waits=[(S.pe, pe_car)], incs=[(S.act, 1)])
        act_car = S.act.n
        w5 = gdma(rsc.ap(), ccar(), waits=[(S.act, act_car)])
        w6 = gdma(crow1(), rsc.ap().rearrange("a b -> (a b)").rearrange("(o x) -> o x", o=1),
                  waits=[(S.gp, w5)])
        if "dbg_crow1" in dbg_d:
            sdma(dbg_d["dbg_crow1"].ap(), crow1(), waits=[(S.gp, w6)])

        p_rk = lambda t: pA.ap()[:, 256 + t * 16: 256 + (t + 1) * 16]
        KM = lambda t: abf[:, 2048 + t * 16: 2048 + (t + 1) * 16]
        KI = lambda t: abf[:, 2304 + t * 16: 2304 + (t + 1) * 16]
        KF = lambda t: abf[:, 2560 + 2 * t: 2560 + 2 * t + 2]
        WK = lambda t: abf[:, 2592 + 2 * t: 2592 + 2 * t + 2]
        C1 = lambda t: abf[:, 2624 + t * 8: 2624 + (t + 1) * 8]
        C2 = lambda t: abf[:, 2752 + t * 8: 2752 + (t + 1) * 8]
        for tt in range(NT):
            P.op("pe", lambda g, t=tt: g.matmul(
                p_rk(t), tri[:], ind12()[:, t * 16:(t + 1) * 16],
                start=True, stop=False),
                waits=[(S.gp, w6), (S.dve, dv_ind)] if tt == 0 else [])
            P.op("pe", lambda g, t=tt: g.matmul(
                p_rk(t), one1r[:], crow1()[0:1, t * 16:(t + 1) * 16],
                start=False, stop=True),
                incs=[(S.pe, 1)] if tt == NT - 1 else [])
        pe_rank = S.pe.n
        dve_all(lambda g, t: g.tensor_scalar(
            KM(t), p_rk(t), float(CAP), None, ALU.is_le),
            waits_first=[(S.pe, pe_rank)], inc_last=False)
        dve_all(lambda g, t: g.tensor_mul(
            KI(t), KM(t), ind12()[:, t * 16:(t + 1) * 16]), inc_last=False)
        dve_all(lambda g, t: g.tensor_reduce(
            KF(t)[:, 0:1], KI(t)[:, 0:8], AX, ALU.add), inc_last=False)
        dve_all(lambda g, t: g.tensor_reduce(
            KF(t)[:, 1:2], KI(t)[:, 8:16], AX, ALU.add), inc_last=False)
        dve_all(lambda g, t: g.tensor_mul(
            WK(t)[:, 0:1], CM1(t), KF(t)[:, 0:1]), inc_last=False)
        dve_all(lambda g, t: g.tensor_mul(
            WK(t)[:, 1:2], CM2(t), KF(t)[:, 1:2]), inc_last=False)
        dve_all(lambda g, t: g.tensor_scalar(
            C1(t), I1(t), WK(t)[:, 0:1], None, ALU.mult), inc_last=False)
        dve_all(lambda g, t: g.tensor_scalar(
            C2(t), I2(t), WK(t)[:, 1:2], None, ALU.mult), inc_last=False)
        dve_all(lambda g, t: g.tensor_add(
            compsb()[:, t * 8:(t + 1) * 8], C1(t), C2(t)))
        dv_comb = S.dve.n
        if "dbg_comb" in dbg_d:
            sdma(dbg_d["dbg_comb"].ap(), compsb(), waits=[(S.dve, dv_comb)])
        if "dbg_logits" in dbg_d:
            sdma(dbg_d["dbg_logits"].ap(), logit(), waits=[(S.dve, dv_comb)])
        if "dbg_xn" in dbg_d:
            sdma(dbg_d["dbg_xn"].ap(), xn.ap()[:], waits=[(S.act, act_xn)])
        P.op("dve", lambda g: g.tensor_copy(compsbB(), compsb()),
             waits=[(S.dve, dv_comb)], incs=[(S.dve, 1)])
        dv_cb = S.dve.n
        for tt in range(NT):
            P.op("pe", lambda g, t=tt: g.transpose(
                pBb[0:8, t * 128:(t + 1) * 128],
                compsbB()[:, t * 8:(t + 1) * 8], idb[:]),
                waits=[(S.dve, dv_cb)] if tt == 0 else [],
                incs=[(S.pe, 1)] if tt == NT - 1 else [])
        pe_ct = S.pe.n
        P.op("act", lambda g: g.copy(combT[:], pBb[0:8, 0:2048]),
             waits=[(S.pe, pe_ct)], incs=[(S.act, 1)])
        act_ct = S.act.n

        barrier()

        if upto < 2:
            with nc.Block() as block:
                P.emit(block)
            return nc
        pe_after_gu = {}
        gu_slot_free = {}
        wd_slot_free = {}
        y_evac = {}
        last_y_dve = 0
        act_gu_evac = {}
        crow_w = {}
        prev_pe_cbB = 0
        prev_act_cbB = 0


        gu_dma = {}
        wd_dma = {}
        dv_prev_gu_all = 0

        for hh in range(m_halves):
            tof = hh * TH
            for e in range(m_experts):
                wcr = gdma(crow[0:1, 0:TH],
                           combT[e:e + 1, tof:tof + TH],
                           waits=[(S.act, act_ct)] if (hh == 0 and e == 0)
                           else [(S.dve, last_y_dve)], sem=S.cr)
                cb_waits = [wcr]
                if prev_act_cbB:
                    cb_waits.append((S.act, S.act.n))
                for ch in range(2):
                    P.op("pe", lambda g, c=ch: g.matmul(
                        p_cbB[:, c * 512:(c + 1) * 512], one1rb[:],
                        crow[0:1, c * 512:(c + 1) * 512],
                        start=True, stop=True),
                        waits=cb_waits if ch == 0 else [],
                        incs=[(S.pe, 1)] if ch == 1 else [])
                pe_cbB = S.pe.n
                P.op("act", lambda g: g.copy(combB[:], p_cbB),
                     waits=[(S.pe, pe_cbB)], incs=[(S.act, 1)])
                prev_act_cbB = S.act.n
                act_cbB = S.act.n

                for it in range(NI):
                    slot = it % 2
                    key = (hh, e, it)
                    wts = []
                    if slot in pe_after_gu:
                        wts.append((S.pe, pe_after_gu[slot]))
                    wg = wdma(wgu_sl(slot * 2), mwgu.ap()[e, it, 0],
                              waits=wts, sem=S_wg[slot])
                    if hh == 0 and e == 0 and it == 0 and "dbg_w" in dbg_d:
                        sdma(dbg_d["dbg_w"].ap(), wgu_sl(0), waits=[wg])
                    wu = wdma(wgu_sl(slot * 2 + 1), mwgu.ap()[e, it, 1],
                              sem=S_wu[slot])
                    gw = [wg, (S.act, act_cbB)]
                    if (hh, e, it - 1) in act_gu_evac:
                        gw.append((S.act, act_gu_evac[(hh, e, it - 1)]))
                    for ch in range(2):
                        for ht in range(HT):
                            P.op("pe", lambda g, c=ch, h=ht, s=slot, o=tof:
                                 g.matmul(
                                p_g[:, c * 512:(c + 1) * 512],
                                wgu_sl(s * 2)[:, h * 128:(h + 1) * 128],
                                xn.ap()[:, h * T + o + c * 512:
                                        h * T + o + (c + 1) * 512],
                                start=(h == 0), stop=(h == HT - 1)),
                                waits=gw if (ch == 0 and ht == 0) else [])
                    for ch in range(2):
                        for ht in range(HT):
                            P.op("pe", lambda g, c=ch, h=ht, s=slot, o=tof:
                                 g.matmul(
                                p_u[:, c * 512:(c + 1) * 512],
                                wgu_sl(s * 2 + 1)[:, h * 128:(h + 1) * 128],
                                xn.ap()[:, h * T + o + c * 512:
                                        h * T + o + (c + 1) * 512],
                                start=(h == 0), stop=(h == HT - 1)),
                                waits=[wu] if (ch == 0 and ht == 0)
                                else [],
                                incs=[(S.pe, 1)] if (ch == 1 and ht == HT - 1)
                                else [])
                    pe_after_gu[slot] = S.pe.n
                    pe_gu = S.pe.n
                    gsl = it % 2
                    gtmp = wsv[:, (4 + gsl) * 1024:(5 + gsl) * 1024]
                    utmp = wsv[:, (6 + gsl) * 1024:(7 + gsl) * 1024]
                    aw2 = [(S.pe, pe_gu)]
                    if (hh, e, it - 2) in gu_dma:
                        aw2.append((S.dve, gu_dma[(hh, e, it - 2)]))
                    elif it < 2 and (e > 0 or hh > 0):
                        aw2.append((S.dve, dv_prev_gu_all))
                    P.op("act", lambda g, gt=gtmp: g.activation(
                        gt, p_g[:, 0:1024], AF.Silu, bias=0.0, scale=1.0),
                        waits=aw2)
                    P.op("act", lambda g, ut=utmp: g.copy(ut, p_u[:, 0:1024]),
                         incs=[(S.act, 1)])
                    act_gu_evac[key] = S.act.n
                    a_gu = S.act.n
                    if hh == 0 and e == 0 and it == 0 and "dbg_gt" in dbg_d:
                        sdma(dbg_d["dbg_gt"].ap()[:, 0:TH], gtmp,
                             waits=[(S.act, a_gu)])
                        sdma(dbg_d["dbg_gt"].ap()[:, TH:2 * TH], utmp,
                             waits=[(S.act, a_gu)])
                    P.op("dve", lambda g, gt=gtmp, ut=utmp, i=it: g.tensor_mul(
                        agv[:, i * TH:(i + 1) * TH], gt, ut),
                        waits=[(S.act, a_gu)])
                    P.op("dve", lambda g: g.drain(fusable=False))
                    P.op("dve", lambda g, i=it: g.tensor_mul(
                        agv[:, i * TH:(i + 1) * TH],
                        agv[:, i * TH:(i + 1) * TH], combB[:]),
                        incs=[(S.dve, 1)])
                    gu_dma[key] = S.dve.n
                dv_gu_all = S.dve.n
                dv_prev_gu_all = dv_gu_all
                if hh == 0 and e == 0:
                    if "dbg_gu" in dbg_d:
                        sdma(dbg_d["dbg_gu"].ap(), agv[:, 0:NI * TH],
                             waits=[(S.dve, dv_gu_all)])
                    if "dbg_cbB" in dbg_d:
                        sdma(dbg_d["dbg_cbB"].ap(), combB[:],
                             waits=[(S.dve, dv_gu_all)])

                for ht in range(HT):
                    slot = ht % 2
                    wts = []
                    if slot in wd_slot_free:
                        wts.append((S.pe, wd_slot_free[slot]))
                    wd = wdma(wd_sl(slot), mwd.ap()[e, ht], waits=wts,
                              sem=S_wd[slot])
                    ysl = ht % 2
                    yw = [wd, (S.dve, dv_gu_all)]
                    if (hh, e, ht - 2) in y_evac:
                        yw.append((S.dve, y_evac[(hh, e, ht - 2)]))
                    elif e > 0 and (hh, e - 1, HT - 2 + (ht % 2)) in y_evac:
                        yw.append((S.dve, y_evac[(hh, e - 1, HT - 2 + (ht % 2))]))
                    for ch in range(2):
                        for it in range(NI):
                            P.op("pe", lambda g, c=ch, i=it, s=slot: g.matmul(
                                p_y(s)[:, c * 512:(c + 1) * 512],
                                wd_sl(s)[:, i * 128:(i + 1) * 128],
                                agv[:, i * TH + c * 512: i * TH + (c + 1) * 512],
                                start=(i == 0), stop=(i == NI - 1)),
                                waits=yw if (ch == 0 and it == 0) else [],
                                incs=[(S.pe, 1)] if (ch == 1 and it == NI - 1)
                                else [])
                    wd_slot_free[slot] = S.pe.n
                    pe_y = S.pe.n
                    ytile = abig.ap()[:, ht * T + tof: ht * T + tof + TH]
                    if e == 0:
                        P.op("dve", lambda g, yt=ytile, s=ysl: g.tensor_copy(
                            yt, p_y(s)), waits=[(S.pe, pe_y)],
                            incs=[(S.dve, 1)])
                    else:
                        P.op("dve", lambda g, yt=ytile, s=ysl: g.tensor_tensor(
                            yt, yt, p_y(s), ALU.add), waits=[(S.pe, pe_y)],
                            incs=[(S.dve, 1)])
                    y_evac[(hh, e, ht)] = S.dve.n
                last_y_dve = S.dve.n
                if hh == 0 and e == 0 and "dbg_y0" in dbg_d:
                    sdma(dbg_d["dbg_y0"].ap(), abig.ap()[:],
                         waits=[(S.dve, last_y_dve)])

        dv_m_done = S.dve.n
        for ht in (range(HT) if (m_halves == 2 and m_experts == E) else []):
            P.op("act", lambda g, h=ht: g.activation(
                abig.ap()[:, h * T:(h + 1) * T], abig.ap()[:, h * T:(h + 1) * T],
                AF.Silu, bias=0.0, scale=1.0),
                waits=[(S.dve, dv_m_done)] if ht == 0 else [],
                incs=[(S.act, 1)])
            a = S.act.n
            sdma(gspill.ap()[ht], abig.ap()[:, ht * T:(ht + 1) * T],
                 waits=[(S.act, a)])
        w_spill = S.sp.n
        if "dbg_y" in dbg_d:
            sdma(dbg_d["dbg_y"].ap(), abig.ap()[:], waits=[(S.act, S.act.n)])

        barrier()

        if upto < 3:
            with nc.Block() as block:
                P.emit(block)
            return nc
        for e in ("act", "dve", "pe", "gp", "sp"):
            P.op(e, lambda g: g.nop(), waits=[(S.sp, S.sp.n)])
        act_xnp = ln_phase(xTp, abig.ap(), do_router=False)
        def head_weights(head, feats, waits):
            ws = []
            for f, slot in feats:
                ws.append(wdma(wq_slot(slot), wqkv.ap()[head, f],
                               waits=waits, sem=S_wq[slot]))
                waits = []
            return ws

        def project(dst, wslot, src_arena, pe_waits, sil):
            for ch in range(4):
                for ht in range(HT):
                    P.op("pe", lambda g, c=ch, h=ht: g.matmul(
                        p_proj[:, c * 512:(c + 1) * 512],
                        wq_slot(wslot)[:, h * 128:(h + 1) * 128],
                        src_arena[:, h * T + c * 512: h * T + (c + 1) * 512],
                        start=(h == 0), stop=(h == HT - 1)),
                        waits=pe_waits if (ch == 0 and ht == 0) else [],
                        incs=[(S.pe, 1)] if (ch == 3 and ht == HT - 1) else [])
            pe_p = S.pe.n
            if sil:
                P.op("act", lambda g, d=dst: g.activation(
                    d, p_proj[:], AF.Silu, bias=0.0, scale=1.0),
                    waits=[(S.pe, pe_p)], incs=[(S.act, 1)])
            else:
                P.op("act", lambda g, d=dst: g.copy(d, p_proj[:]),
                     waits=[(S.pe, pe_p)], incs=[(S.act, 1)])
            return S.act.n, pe_p

        def head_decays(head, need_q):
            nscol = nsl[:, head:head + 1]
            if "dec" in skip:
                P.op("act", lambda g: g.nop(), incs=[(S.act, 1)], no_auto=True)
                return S.act.n
            if need_q:
                P.op("act", lambda g, ns=nscol: g.activation(
                    decq(), a1r[:], AF.Exp, bias=0.0, scale=ns))
                P.op("act", lambda g, ns=nscol: g.activation(
                    dect()[:, 0:BLK], m1c[:], AF.Exp, bias=0.0, scale=ns))
                P.op("act", lambda g, ns=nscol: g.activation(
                    dect()[:, BLK:2 * BLK], m2c[:], AF.Exp, bias=0.0, scale=ns))
            P.op("act", lambda g, ns=nscol: g.activation(
                deck[:], a2c[:], AF.Exp, bias=0.0, scale=ns))
            P.op("act", lambda g, ns=nscol: g.activation(
                bdcol[:], c256c[:], AF.Exp, bias=0.0, scale=ns),
                incs=[(S.act, 1)])
            return S.act.n

        qkd_carry = [0]

        def scan_blocks(head, src_k, src_v, with_q, seed_from, store_seed,
                        act_kv_ready, o_dst=None, nblk=NBLK):
            iw = [(S.act, S.act.n)]
            if seed_from is None:
                P.op("dve", lambda g: g.memset(kvf[:], 0.0), waits=iw)
                P.op("dve", lambda g: g.memset(kvb[:, 0:DH], 0.0),
                     incs=[(S.dve, 1)])
            else:
                P.op("dve", lambda g, s=seed_from: g.tensor_copy(kvf[:], s),
                     waits=iw)
                P.op("dve", lambda g, s=seed_from: g.tensor_copy(
                    kvb[:, 0:DH], s), incs=[(S.dve, 1)])
            dv_kv = S.dve.n
            act_cast = S.act.n
            prev_tr_evac = 0
            prev_o_evac = {}
            prev_qkd = qkd_carry[0]
            st1 = {}
            pe_o_last = [0]

            def stage1(j):
                nonlocal prev_tr_evac, prev_qkd
                sl = j % 2
                t0 = j * BLK
                r = {}
                if "tr" not in skip:
                    trw = [(S.act, act_kv_ready)]
                    if prev_tr_evac:
                        trw = [(S.dve, prev_tr_evac[0]),
                               (S.act, prev_tr_evac[1])]
                    for ss in range(2):
                        P.op("pe", lambda g, s=ss, t=t0: g.transpose(
                            p_tr[:, s * 128:(s + 1) * 128],
                            src_k[:, t + s * 128: t + (s + 1) * 128], idb[:]),
                            waits=trw if ss == 0 else [])
                        trw = []
                    for ss in range(2):
                        P.op("pe", lambda g, s=ss, t=t0: g.transpose(
                            p_trv[:, s * 128: (s + 1) * 128],
                            src_v[:, t + s * 128: t + (s + 1) * 128], idb[:]),
                            incs=[(S.pe, 1)] if ss == 1 else [])
                    pe_tr = S.pe.n
                    if "evk" in skip:
                        P.op("dve", lambda g: g.engine_nop(),
                             waits=[(S.pe, pe_tr)], incs=[(S.dve, 1)])
                    else:
                        for ss in range(2):
                            P.op("dve", lambda g, s=ss, q=sl: g.tensor_scalar(
                                ktm(q)[:, s * 128:(s + 1) * 128],
                                p_tr[:, s * 128:(s + 1) * 128],
                                deck[:, s:s + 1], None, ALU.mult),
                                waits=[(S.pe, pe_tr)] if ss == 0 else [],
                                incs=[(S.dve, 1)] if ss == 1 else [])
                    r["dv_ktm"] = S.dve.n
                    if "evv" in skip:
                        P.op("act", lambda g: g.nop(),
                             waits=[(S.pe, pe_tr)], incs=[(S.act, 1)],
                             no_auto=True)
                    else:
                        for ss in range(2):
                            P.op("act", lambda g, s=ss, q=sl: g.copy(
                                vtok(q)[:, s * 128:(s + 1) * 128],
                                p_trv[:, s * 128: (s + 1) * 128]),
                                waits=[(S.pe, pe_tr)] if ss == 0 else [],
                                incs=[(S.act, 1)] if ss == 1 else [])
                    r["act_vt"] = S.act.n
                    prev_tr_evac = (r["dv_ktm"], r["act_vt"])
                if with_q:
                    qw = [(S.pe, pe_o_last[0])] if pe_o_last[0] else []
                    P.op("dve", lambda g, q=sl, t=t0: g.tensor_mul(
                        qdb(q), qtile()[:, t:t + BLK], decq()),
                        waits=qw, incs=[(S.dve, 1)])
                    r["dv_qd"] = S.dve.n
                    qkw = []
                    if prev_qkd:
                        qkw.append((S.dve, prev_qkd))
                    for ss in range(2):
                        P.op("pe", lambda g, s=ss, t=t0: g.matmul(
                            p_qk[:, s * 256:(s + 1) * 256],
                            src_k[:, t + s * 128: t + (s + 1) * 128],
                            qtile()[:, t:t + BLK], start=True, stop=True),
                            waits=qkw if ss == 0 else [],
                            incs=[(S.pe, 1)] if ss == 1 else [])
                    pe_qk = S.pe.n
                    for ss in range(2):
                        P.op("dve", lambda g, s=ss, q=sl: g.tensor_tensor(
                            qkdb(q)[:, s * 256:(s + 1) * 256],
                            p_qk[:, s * 256:(s + 1) * 256],
                            dect()[:, s * 256:(s + 1) * 256], ALU.mult),
                            waits=[(S.pe, pe_qk)] if ss == 0 else [],
                            incs=[(S.dve, 1)] if ss == 1 else [])
                    prev_qkd = S.dve.n
                    qkd_carry[0] = S.dve.n
                    r["dv_qkd"] = S.dve.n
                return r

            st1[0] = stage1(0)
            for j in range(nblk):
                sl = j % 2
                t0 = j * BLK
                if j + 1 < nblk:
                    st1[j + 1] = stage1(j + 1)
                r = st1[j]
                if with_q and "tr" not in skip:
                    ow = [(S.dve, r["dv_qd"]), (S.act, act_cast)]
                    if j >= 2 and (j - 2) in prev_o_evac:
                        ow.append((S.act, prev_o_evac[j - 2]))
                    P.op("pe", lambda g, q=sl: g.matmul(
                        p_o(q), kvb[:, q * DH:(q + 1) * DH], qdb(q),
                        start=True, stop=False), waits=ow)
                    for ss in range(2):
                        P.op("pe", lambda g, s=ss, q=sl: g.matmul(
                            p_o(q), vtok(q)[:, s * 128:(s + 1) * 128],
                            qkdb(q)[:, s * 256:(s + 1) * 256],
                            start=False, stop=(s == 1)),
                            waits=[(S.dve, r["dv_qkd"]), (S.dve, r["dv_ktm"]),
                                   (S.act, r["act_vt"])] if ss == 0 else [],
                            incs=[(S.pe, 1)] if ss == 1 else [])
                    pe_o = S.pe.n
                    pe_o_last[0] = pe_o
                    P.op("act", lambda g, q=sl, t=t0, od=o_dst: g.copy(
                        od[:, t:t + BLK], p_o(q)),
                        waits=[(S.pe, pe_o)], incs=[(S.act, 1)])
                    prev_o_evac[j] = S.act.n
                if "tr" in skip or "amm" in skip:
                    continue
                aw = [(S.dve, r["dv_ktm"]), (S.act, r["act_vt"]),
                      (S.dve, dv_kv)]
                for ss in range(2):
                    P.op("pe", lambda g, s=ss, q=sl: g.matmul(
                        p_A2, ktm(q)[:, s * 128:(s + 1) * 128],
                        vtok(q)[:, s * 128:(s + 1) * 128],
                        start=(s == 0), stop=(s == 1)),
                        waits=aw if ss == 0 else [],
                        incs=[(S.pe, 1)] if ss == 1 else [])
                pe_A = S.pe.n
                if "kvu" in skip:
                    continue
                P.op("dve", lambda g: g.drain(fusable=False))
                P.op("dve", lambda g: g.scalar_tensor_tensor(
                    kvf[:], kvf[:], bdcol[:, 0:1], p_A2, ALU.mult, ALU.add),
                    waits=[(S.pe, pe_A)], incs=[(S.dve, 1)])
                dv_kv = S.dve.n
                P.op("act", lambda g, q=(j + 1) % 2: g.copy(
                    kvb[:, q * DH:(q + 1) * DH], kvf[:]),
                    waits=[(S.dve, dv_kv)], incs=[(S.act, 1)])
                act_cast = S.act.n
            if store_seed is not None and "seed" not in skip:
                P.op("act", lambda g, d=store_seed: g.copy(d, kvf[:]),
                     waits=[(S.dve, dv_kv)], incs=[(S.act, 1)])
            return S.act.n

        prev_proj_pe = 0
        for head in range(a0_heads):
            slk = (head % 2) * 2
            wts = [(S.act, act_xnp)] if head == 0 else []
            if head >= 2:
                wts.append((S.pe, prev_proj_pe))
            wk = head_weights(head, [(1, slk), (2, slk + 1)], wts)
            dec_a = head_decays(head, need_q=False)
            pw = [wk[0], wk[1]]
            if head > 0:
                pw.append((S.act, S.act.n))
                pw.append((S.dve, S.dve.n))
            a_k, _ = project(ktile(), slk, abig.ap(), pw, sil=True)
            a_v, pe_v = project(vtile(), slk + 1, abig.ap(), [(S.act, a_k)],
                                sil=False)
            prev_proj_pe = pe_v
            scan_blocks(head, ktile(), vtile(), with_q=False,
                        seed_from=None,
                        store_seed=kvseed[:, head * DH:(head + 1) * DH],
                        act_kv_ready=a_v, nblk=a0_blocks)
        if "dbg_seed" in dbg_d:
            sdma(dbg_d["dbg_seed"].ap(), kvseed[:], waits=[(S.act, S.act.n)])

        barrier()

        if upto < 4:
            with nc.Block() as block:
                P.emit(block)
            return nc
        prev_proj_pe = 0
        for head in range(a_heads):
            slk = (head % 3)
            wts = []
            if head >= 1:
                wts.append((S.pe, prev_proj_pe))
            wq = head_weights(head, [(0, 0), (1, 1), (2, 2)], wts)
            dec_a = head_decays(head, need_q=True)
            pw = [wq[0], wq[1], wq[2]]
            if head > 0:
                pw.append((S.act, S.act.n))
                pw.append((S.dve, S.dve.n))
            a_q, _ = project(qtile(), 0, xn.ap(), pw, sil=True)
            a_k, _ = project(ktile(), 1, xn.ap(), [(S.act, a_q)], sil=True)
            a_v, pe_v = project(vtile(), 2, xn.ap(), [(S.act, a_k)], sil=False)
            prev_proj_pe = pe_v
            o_dst = abig.ap()[:, head * T:(head + 1) * T]
            scan_blocks(head, ktile(), vtile(), with_q=True,
                        seed_from=kvseed[:, head * DH:(head + 1) * DH],
                        store_seed=None, act_kv_ready=a_v, o_dst=o_dst)
        act_o_done = S.act.n
        if "dbg_o" in dbg_d:
            sdma(dbg_d["dbg_o"].ap(), abig.ap()[:], waits=[(S.act, act_o_done)])

        barrier()

        if upto < 5:
            with nc.Block() as block:
                P.emit(block)
            return nc
        dv_sq = {}
        for ht in range(HT):
            sl = ht % 2
            sq_w = []
            if ht >= 2:
                sq_w.append((S.pe, dv_sq[ht - 2]))
            P.op("dve", lambda g, h=ht, s=sl: g.tensor_mul(
                sqg(s), abig.ap()[:, h * T:(h + 1) * T],
                abig.ap()[:, h * T:(h + 1) * T]),
                waits=sq_w, incs=[(S.dve, 1)])
            dvn = S.dve.n
            for ch in range(4):
                P.op("pe", lambda g, c=ch, h=ht, s=sl: g.matmul(
                    p_r[:, c * 512:(c + 1) * 512], onesb[:],
                    sqg(s)[:, c * 512:(c + 1) * 512],
                    start=(h == 0), stop=(h == HT - 1)),
                    waits=[(S.dve, dvn)] if ch == 0 else [],
                    incs=[(S.pe, 1)] if ch == 3 else [])
            dv_sq[ht] = S.pe.n
        pe_r = S.pe.n
        for ch in range(4):
            P.op("dve", lambda g, c=ch: g.tensor_scalar(
                astr2.ap()[:, c * 512:(c + 1) * 512],
                p_r[:, c * 512:(c + 1) * 512], 1.0 / 2048.0, None, ALU.mult),
                waits=[(S.pe, pe_r)] if ch == 0 else [],
                incs=[(S.dve, 1)] if ch == 3 else [])
        dv_rt = S.dve.n
        P.op("act", lambda g: g.activation(
            astr.ap()[:], astr2.ap()[:], AF.Ln, bias=1e-6, scale=1.0),
            waits=[(S.dve, dv_rt)])
        P.op("act", lambda g: g.drain(fusable=False))
        P.op("act", lambda g: g.activation(
            rbv(), astr.ap()[:], AF.Exp, bias=0.0, scale=-0.5),
            incs=[(S.act, 1)])
        act_rb = S.act.n

        prev_upe = 0
        prev_act_u = 0
        dve_gu = {}
        for ht in range(HT):
            sl = ht % 2
            wts = [(S.act, act_rb)] if ht == 0 else []
            if ht >= 2:
                wts.append((S.pe, prev_upe))
            wu_ = wdma(uwg(sl), wuat.ap()[ht], waits=wts, sem=S_ug[sl])
            uw = [wu_]
            if prev_act_u:
                uw.append((S.act, prev_act_u))
            else:
                uw.append((S.act, act_rb))
            if ht >= 2:
                uw.append((S.dve, dve_gu[ht - 2]))
            for ch in range(4):
                for h2 in range(HT):
                    P.op("pe", lambda g, c=ch, h=h2, s=sl: g.matmul(
                        p_ug[:, c * 512:(c + 1) * 512],
                        uwg(s)[:, h * 128:(h + 1) * 128],
                        xn.ap()[:, h * T + c * 512: h * T + (c + 1) * 512],
                        start=(h == 0), stop=(h == HT - 1)),
                        waits=uw if (ch == 0 and h2 == 0) else [],
                        incs=[(S.pe, 1)] if (ch == 3 and h2 == HT - 1) else [])
            prev_upe = S.pe.n
            pe_u = S.pe.n
            P.op("act", lambda g, s=sl: g.copy(utg(s), p_ug[:]),
                 waits=[(S.pe, pe_u)], incs=[(S.act, 1)])
            prev_act_u = S.act.n
            act_u = S.act.n
            P.op("dve", lambda g, h=ht, s=sl: g.tensor_mul(
                gutv(s), abig.ap()[:, h * T:(h + 1) * T], rbv()),
                waits=[(S.act, act_u)])
            P.op("dve", lambda g: g.drain(fusable=False))
            P.op("dve", lambda g, h=ht, s=sl: g.tensor_mul(
                abig.ap()[:, h * T:(h + 1) * T], gutv(s), utg(s)),
                incs=[(S.dve, 1)])
            dve_gu[ht] = S.dve.n
        dv_gu_att = S.dve.n

        prev_spe = 0
        prev_out_dve = 0
        prev_store = 0
        for ot in range(HT):
            sl = ot % 2
            wts = [(S.dve, dv_gu_att)] if ot == 0 else []
            if ot >= 2:
                wts.append((S.pe, prev_spe))
            wo = wdma(woutg(sl), wout.ap()[ot], waits=wts, sem=S_wo[sl])
            wg_ = sdma(astr.ap()[:], xT.ap()[ot * 128:(ot + 1) * 128, :],
                       waits=[(S.dve, prev_out_dve)] if ot >= 1 else
                       [(S.act, act_rb)], sem=S.x)
            wb_ = gdma(gbk(), gspill.ap()[ot],
                       waits=[(S.dve, prev_out_dve)] if ot >= 1 else
                       [(S.sp, w_spill)], sem=S.gb)
            sw = [wo, (S.dve, dv_gu_att)]
            if ot >= 1:
                sw.append((S.dve, prev_out_dve))
            for ch in range(4):
                for h2 in range(HT):
                    P.op("pe", lambda g, c=ch, h=h2, s=sl: g.matmul(
                        p_s[:, c * 512:(c + 1) * 512],
                        woutg(s)[:, h * 128:(h + 1) * 128],
                        abig.ap()[:, h * T + c * 512: h * T + (c + 1) * 512],
                        start=(h == 0), stop=(h == HT - 1)),
                        waits=sw if (ch == 0 and h2 == 0) else [],
                        incs=[(S.pe, 1)] if (ch == 3 and h2 == HT - 1) else [])
            prev_spe = S.pe.n
            pe_s = S.pe.n
            dw = [(S.pe, pe_s), wb_, wg_]
            if ot == 0:
                dw.append((S.act, act_rb))
            else:
                dw.append((S.sp, prev_store))
            P.op("dve", lambda g: g.tensor_tensor(
                astr2.ap()[:], p_s[:], gbk(), ALU.mult), waits=dw)
            P.op("dve", lambda g: g.drain(fusable=False))
            P.op("dve", lambda g: g.tensor_add(
                astr2.ap()[:], astr2.ap()[:], astr.ap()[:]),
                incs=[(S.dve, 1)])
            prev_out_dve = S.dve.n
            prev_store = sdma(out_d.ap()[ot * 128:(ot + 1) * 128, :],
                              astr2.ap()[:], waits=[(S.dve, prev_out_dve)])
        w_out_done = S.sp.n
        P.op("sp", lambda g: g.nop(), waits=[(S.sp, w_out_done)])

        with nc.Block() as block:
            P.emit(block)

    return nc



def make_in_maps(L0_attn_x, L0_attn_slope_rate, L0_attn_W_qkvu, L0_attn_W_out,
                 L0_ffn_Wr, L0_ffn_Wg, L0_ffn_Wu, L0_ffn_Wd):
    x = np.asarray(L0_attn_x, dtype=np.float32)
    shared = stage_shared(L0_attn_W_qkvu, L0_attn_W_out, L0_ffn_Wr,
                          L0_ffn_Wg, L0_ffn_Wu, L0_ffn_Wd,
                          L0_attn_slope_rate)
    consts = stage_consts()
    zeros = np.zeros((D, T), np.float32)
    in_maps = []
    for c in range(8):
        b, half = c // 2, c % 2
        m = dict(shared)
        m.update(consts)
        m["xT"] = np.ascontiguousarray(x[b, half * T:(half + 1) * T, :].T)
        m["xTp"] = (np.ascontiguousarray(x[b, 0:T, :].T) if half == 1
                    else zeros)
        m["below"] = (np.arange(8) < c).astype(np.float32).reshape(8, 1)
        in_maps.append(m)
    return in_maps


def kernel(L0_attn_x, L0_attn_slope_rate, L0_attn_W_qkvu, L0_attn_W_out,
           L0_ffn_Wr, L0_ffn_Wg, L0_ffn_Wu, L0_ffn_Wd):
    in_maps = make_in_maps(L0_attn_x, L0_attn_slope_rate, L0_attn_W_qkvu,
                           L0_attn_W_out, L0_ffn_Wr, L0_ffn_Wg, L0_ffn_Wu,
                           L0_ffn_Wd)
    nc = build_nc()
    res = run_bass_kernel_spmd(nc, in_maps, core_ids=list(range(8)))
    out = np.empty((4, 4096, D), np.float32)
    for c in range(8):
        b, half = c // 2, c % 2
        out[b, half * T:(half + 1) * T, :] = res.results[c]["out"].T
    return out
